# version 09
# baseline (speedup 1.0000x reference)
import math
from contextlib import ExitStack

import numpy as np
import ml_dtypes
import concourse.bass as bass
import concourse.mybir as mybir
from concourse.bass_utils import run_bass_kernel_spmd

F32 = mybir.dt.float32
BF16 = mybir.dt.bfloat16
I32 = mybir.dt.int32
AL = mybir.AluOpType
AF = mybir.ActivationFunctionType

S = 2048
H = 2048
NH = 16
HD = 128
NCH = 16
NQB = 4
QS = 512
NSB = 4
SCALE = 1.0 / math.sqrt(HD)
CSHIFT = -20.0
EPS_LN = 1e-5
EPS_FFN = 1e-10
PI = math.pi
TWO_PI = 2.0 * math.pi
_C1 = float(np.float32(6.28125))
_C2 = float(np.float32(np.float64(TWO_PI) - np.float64(_C1)))
_C3 = float(np.float32(np.float64(TWO_PI) - np.float64(_C1) - np.float64(_C2)))
LN1E4_64 = math.log(10000.0) / 64.0

DEBUG_OUT = None


class Sem:
    def __init__(self, handle):
        self.h = handle
        self.n = 0

    def inc(self, k=1):
        self.n += k
        return self.n


def build_program():
    nc = bass.Bass("TRN2", target_bir_lowering=False)

    xt = nc.dram_tensor("xt", [NCH, 128, S], F32, kind="ExternalInput")
    wq = nc.dram_tensor("wq", [NH, 128, NCH, 128], F32, kind="ExternalInput")
    wk = nc.dram_tensor("wk", [NH, 128, NCH, 128], F32, kind="ExternalInput")
    wv = nc.dram_tensor("wv", [NH, 128, NCH, 128], F32, kind="ExternalInput")
    wo = nc.dram_tensor("wo", [NCH, 128, NCH, 128], F32, kind="ExternalInput")
    wf = nc.dram_tensor("wf", [NCH, 128, NCH, 128], F32, kind="ExternalInput")
    bias_d = nc.dram_tensor("bias_d", [128, NCH], F32, kind="ExternalInput")
    yt = nc.dram_tensor("yt", [NCH, 128, S], F32, kind="ExternalOutput")
    dbg = None
    if DEBUG_OUT is not None:
        dbg = nc.dram_tensor("dbg", [128, NCH, S], F32, kind="ExternalOutput")

    events = []

    def ev(eng, fn):
        events.append((eng, fn))

    def wait(eng, sem, v):
        if v is None or v <= 0:
            return
        assert v <= sem.n, f"wait on future value {v} > {sem.n}"
        ev(eng, lambda e, h=sem.h, vv=v: e.wait_ge(h, vv))

    def dma(eng, sem, dst, src, slow=False):
        v = sem.inc(16)
        ev(eng, lambda e, d=dst, s_=src, h=sem.h, sl_=slow:
           e.dma_start(d, s_, allow_slow_non_contiguous=sl_).then_inc(h, 16))
        return v

    def op(eng, fn, sem=None):
        if sem is None:
            ev(eng, fn)
            return None
        v = sem.inc(1)
        ev(eng, lambda e, f=fn, h=sem.h: f(e).then_inc(h, 1))
        return v

    with ExitStack() as st:
        def sb(name, shape, dt):
            return st.enter_context(nc.sbuf_tensor(name, shape, dt))

        def ps(name, shape, dt=F32):
            return st.enter_context(nc.psum_tensor(name, shape, dt))

        XN = sb("XN", [128, NCH, S], BF16)
        AH = sb("AH", [128, NCH, S], BF16)
        WS = sb("WS", [128, S], F32)
        WS2 = sb("WS2", [128, S], F32)
        RB = sb("RB", [128, S], F32)
        WQB = sb("WQB", [128, S], BF16)
        WKB = sb("WKB", [128, S], BF16)
        WVB = sb("WVB", [128, S], BF16)
        QR = sb("QR", [128, S], BF16)
        QSW = sb("QSW", [128, S], BF16)
        QF = sb("QF", [128, S], BF16)
        KF = sb("KF", [128, S], BF16)
        VH = sb("VH", [128, NCH, 128], BF16)
        PT = sb("PT", [128, 4, QS], BF16)
        PTX = sb("PTX", [128, 2, 128], BF16)
        RRB = sb("RRB", [128, 2, QS], F32)
        KI = sb("KI", [128, 256], I32)
        LROW = sb("LROW", [1, QS], F32)
        TA = sb("TA", [128, S], BF16)
        TB = sb("TB", [128, S], BF16)
        onescol = sb("onescol", [128, 1], BF16)
        onescolf = sb("onescolf", [128, 1], F32)
        ones1 = sb("ones1", [1, 128], F32)
        tri = sb("tri", [128, 128], BF16)
        ident = sb("ident", [128, 128], BF16)
        bias_sb = sb("bias_sb", [128, NCH], F32)
        J64 = sb("J64", [128, 1], F32)
        INV = sb("INV", [128, 1], F32)
        SGN = sb("SGN", [128, 1], F32)
        NEGC = sb("NEGC", [128, 1], F32)
        EPSV = sb("EPSV", [1, 1], F32)
        LN2V = sb("LN2V", [1, 1], F32)
        ACC = sb("ACC", [128, 1], F32)
        ACC2 = sb("ACC2", [128, 1], F32)
        ACC3 = sb("ACC3", [128, 1], F32)
        SVAL = sb("SVAL", [1, 8], F32)
        TPOS = sb("TPOS", [128, 1], F32)
        TNEG = sb("TNEG", [128, 1], F32)
        S2K = sb("S2K", [128, 1], F32)
        I2S = sb("I2S", [128, 1], F32)

        P = [ps(f"P{i}", [128, 512]) for i in range(4)]
        PS4 = ps("PS4", [128, 512])
        PS5B = ps("PS5B", [128, 1024], BF16)
        PPV = ps("PPV", [128, 512])
        PL = ps("PL", [128, 512])

        io = Sem(st.enter_context(nc.semaphore("io")))
        iog = Sem(st.enter_context(nc.semaphore("iog")))
        ios = Sem(st.enter_context(nc.semaphore("ios")))
        tS = Sem(st.enter_context(nc.semaphore("tS")))
        vS = Sem(st.enter_context(nc.semaphore("vS")))
        aS = Sem(st.enter_context(nc.semaphore("aS")))
        gS = Sem(st.enter_context(nc.semaphore("gS")))

        VT = QSW
        xstage = [WS, WS2]

        def gp_setup(e):
            e.iota(WS2[:], [[1, S]], channel_multiplier=0, allow_small_or_imprecise_dtypes=True)
            e.iota(J64[0:64, :], [[0, 1]], channel_multiplier=1, allow_small_or_imprecise_dtypes=True)
            e.iota(J64[64:128, :], [[0, 1]], channel_multiplier=1, allow_small_or_imprecise_dtypes=True)
            e.memset(SGN[0:64, :], -1.0)
            e.memset(SGN[64:128, :], 1.0)
            e.memset(onescol[:], 1.0)
            e.memset(onescolf[:], 1.0)
            e.memset(ones1[:], 1.0)
            e.memset(NEGC[:], CSHIFT)
            e.memset(EPSV[:], EPS_LN)
            e.memset(LN2V[:], math.log(2.0))
            e.memset(QF[:, 0:128], 1.0)
            e.affine_select(tri[:], QF[:, 0:128], [[1, 128]], AL.is_ge, 0.0,
                            base=0, channel_multiplier=-1)
            return e.affine_select(ident[:], QF[:, 0:128], [[1, 128]], AL.is_equal, 0.0,
                                   base=0, channel_multiplier=-1)
        g_setup = op("gpsimd", gp_setup, gS)
        b_bias = dma("sync", io, bias_sb[:], bias_d[:])

        ld_w = [[None, None, None] for _ in range(NH)]
        wait("gpsimd", gS, g_setup)
        ld_w[0][0] = dma("gpsimd", iog, WQB[:], wq[0])
        ld_w[0][1] = dma("gpsimd", iog, WKB[:], wk[0])
        ld_w[0][2] = dma("gpsimd", iog, WVB[:], wv[0])

        wait("scalar", gS, g_setup)
        a_inv = op("scalar", lambda e: e.activation(INV[:], J64[:], AF.Exp, scale=-LN1E4_64), aS)
        wait("vector", gS, g_setup)
        wait("vector", aS, a_inv)
        op("vector", lambda e: e.tensor_scalar(RB[:], WS2[:], INV[:], None, AL.mult))
        for cc in range(8):
            sl = slice(256 * cc, 256 * cc + 256)
            op("vector", lambda e, sl=sl: e.tensor_scalar(WS2[:, sl], RB[:, sl], 1.0 / TWO_PI, None, AL.mult))
            op("vector", lambda e, sl=sl: e.tensor_copy(KI[:], WS2[:, sl]))
            op("vector", lambda e, sl=sl: e.tensor_copy(WS2[:, sl], KI[:]))
        op("vector", lambda e: e.scalar_tensor_tensor(WS[:], WS2[:], _C1, RB[:], AL.mult, AL.subtract))
        op("vector", lambda e: e.scalar_tensor_tensor(RB[:], WS2[:], _C2, WS[:], AL.mult, AL.add))
        op("vector", lambda e: e.scalar_tensor_tensor(WS[:], WS2[:], _C3, RB[:], AL.mult, AL.add))
        op("vector", lambda e: e.tensor_scalar(WS2[:], WS[:], -PI, TWO_PI, AL.is_lt, AL.mult))
        op("vector", lambda e: e.tensor_add(RB[:], WS[:], WS2[:]))
        op("vector", lambda e: e.tensor_scalar(WS2[:], RB[:], PI, TWO_PI, AL.is_gt, AL.mult))
        v_rs = op("vector", lambda e: e.tensor_sub(WS[:], RB[:], WS2[:]), vS)
        wait("scalar", vS, v_rs)
        a_sin = op("scalar", lambda e: e.activation(TA[:], WS[:], AF.Sin, scale=-1.0), aS)
        wait("vector", aS, a_sin)
        op("vector", lambda e: e.tensor_scalar(RB[:], WS[:], PI / 2.0, None, AL.add))
        op("vector", lambda e: e.tensor_scalar(WS2[:], RB[:], PI, TWO_PI, AL.is_gt, AL.mult))
        v_rc = op("vector", lambda e: e.tensor_sub(WS[:], RB[:], WS2[:]), vS)
        wait("scalar", vS, v_rc)
        a_cos = op("scalar", lambda e: e.activation(QR[:], WS[:], AF.Sin), aS)
        wait("vector", aS, a_cos)
        v_tab = op("vector", lambda e: e.tensor_scalar(TB[:], QR[:], SGN[:], None, AL.mult), vS)

        d_x = [None] * NCH
        v_cast = [None] * NCH
        wait("sync", vS, v_tab)
        d_x[0] = dma("sync", io, WS[:], xt[0])
        d_x[1] = dma("sync", io, WS2[:], xt[1])
        for ic in range(NCH):
            wait("vector", io, d_x[ic])
            v_cast[ic] = op("vector", lambda e, ic=ic: e.tensor_copy(AH[:, ic, :], xstage[ic % 2][:]), vS)
            if ic + 2 < NCH:
                wait("sync", vS, v_cast[ic])
                d_x[ic + 2] = dma("sync", io, xstage[ic % 2][:], xt[ic + 2])
        v_ep = None
        for sbk in range(NSB):
            sl = slice(512 * sbk, 512 * sbk + 512)
            sq_marks = [None, None]
            for ic in range(NCH):
                wait("scalar", tS, sq_marks[ic % 2])
                if ic == 0:
                    wait("scalar", vS, v_cast[NCH - 1])
                    if v_ep is not None:
                        wait("scalar", vS, v_ep)
                a_sq = op("scalar", lambda e, ic=ic, sl=sl:
                          e.activation(PT[:, ic % 2, :], AH[:, ic, sl], AF.Square), aS)
                wait("tensor", aS, a_sq)
                st_, sp_ = (ic == 0), (ic == NCH - 1)
                op("tensor", lambda e, ic=ic, sl=sl, st_=st_, sp_=sp_:
                   e.matmul(PS4[0:1, :], onescol[:], AH[:, ic, sl], start=st_, stop=sp_))
                sq_marks[ic % 2] = op("tensor", lambda e, ic=ic, st_=st_, sp_=sp_:
                                      e.matmul(PL[0:1, :], onescol[:], PT[:, ic % 2, :],
                                               start=st_, stop=sp_), tS)
            t_mm = sq_marks[(NCH - 1) % 2]
            wait("vector", tS, t_mm)
            op("vector", lambda e: e.tensor_scalar(LROW[:], PS4[0:1, :], 1.0 / H, None, AL.mult))
            op("vector", lambda e: e.tensor_scalar(RRB[0:1, 0, :], PL[0:1, :], 1.0 / H, None, AL.mult))
            op("vector", lambda e: e.tensor_mul(PS4[0:1, :], LROW[:], LROW[:]))
            v_var = op("vector", lambda e: e.tensor_sub(RRB[0:1, 0, :], RRB[0:1, 0, :], PS4[0:1, :]), vS)
            wait("scalar", vS, v_var)
            op("scalar", lambda e: e.activation(RRB[0:1, 0, :], RRB[0:1, 0, :], AF.Ln, bias=EPSV[:]))
            a_r = op("scalar", lambda e: e.activation(RRB[0:1, 0, :], RRB[0:1, 0, :], AF.Exp, scale=-0.5), aS)
            wait("tensor", aS, a_r)
            op("tensor", lambda e, sbk=sbk: e.matmul(P[sbk][:], ones1[:], LROW[:], start=True, stop=True))
            t_r = op("tensor", lambda e: e.matmul(PS4[:], ones1[:], RRB[0:1, 0, :], start=True, stop=True), tS)
            wait("vector", tS, t_r)
            v_ep = op("vector", lambda e, sl=sl: e.tensor_copy(RB[:, sl], PS4[:]), vS)
            wait("tensor", vS, v_ep)

        d_x = [None] * NCH
        wait("sync", vS, v_ep)
        d_x[0] = dma("sync", io, WS[:], xt[0])
        d_x[1] = dma("sync", io, WS2[:], xt[1])
        v_xn = [None] * NCH
        for ic in range(NCH):
            wait("vector", io, d_x[ic])
            for sbk in range(NSB):
                sl = slice(512 * sbk, 512 * sbk + 512)
                op("vector", lambda e, ic=ic, sl=sl, sbk=sbk:
                   e.tensor_sub(RRB[:, 1, :], xstage[ic % 2][:, sl], P[sbk][:]))
                v_xn[ic] = op("vector", lambda e, ic=ic, sl=sl:
                              e.tensor_mul(XN[:, ic, sl], RRB[:, 1, :], RB[:, sl]), vS)
            if ic + 2 < NCH:
                wait("sync", vS, v_xn[ic])
                d_x[ic + 2] = dma("sync", io, xstage[ic % 2][:], xt[ic + 2])
        v_apply = v_xn[NCH - 1]

        ffn_d = [None] * NCH
        ffn_abs = [None] * NCH
        ffn_acc_v = [None]

        def emit_ffn_stat(ot):
            if ot < 2:
                wait("sync", vS, v_apply)
            else:
                wait("sync", aS, ffn_abs[ot - 2])
            ffn_d[ot] = dma("sync", io, xstage[ot % 2][:], wf[ot])
            wait("scalar", io, ffn_d[ot])
            for pc in range(4):
                psl = slice(512 * pc, 512 * pc + 512)
                first = (ot == 0 and pc == 0)
                acc_t = ACC if first else (ACC2 if pc % 2 == 0 else ACC3)
                if not first:
                    wait("scalar", gS, ffn_acc_v[0])
                ffn_abs[ot] = op("scalar", lambda e, ot=ot, psl=psl, acc_t=acc_t:
                                 e.activation(RRB[:, 1, :], xstage[ot % 2][:, psl], AF.Abs,
                                              accum_out=acc_t[:]), aS)
                if not first:
                    wait("gpsimd", aS, ffn_abs[ot])
                    ffn_acc_v[0] = op("gpsimd", lambda e, acc_t=acc_t:
                                      e.tensor_add(ACC[:], ACC[:], acc_t[:]), gS)

        bank_free_v = [None] * 4
        bank_exp_a = [None] * 4

        def wait_bf(eng, bf):
            if bf is not None:
                wait(eng, bf[0], bf[1])
        exp_marks = {}
        pv_marks = {}
        vm_marks = {}
        chunk_info = {}
        g_ctr = 0
        v_ah_last = None
        t_bc_pending = None

        def proj(wbuf, marks, ldmark):
            wait("tensor", iog, ldmark)
            for pair in range(2):
                banks = (2 * pair, 2 * pair + 1)
                for sbk in banks:
                    wait_bf("tensor", bank_free_v[sbk])
                    wait("tensor", aS, bank_exp_a[sbk])
                    bank_free_v[sbk] = None
                    bank_exp_a[sbk] = None
                for ic in range(NCH):
                    st_, sp_ = (ic == 0), (ic == NCH - 1)
                    for sbk in banks:
                        sl = slice(512 * sbk, 512 * sbk + 512)
                        m = op("tensor",
                               lambda e, wbuf=wbuf, ic=ic, sl=sl, sbk=sbk, st_=st_, sp_=sp_:
                               e.matmul(P[sbk][:], wbuf[:, 128 * ic:128 * ic + 128],
                                        XN[:, ic, sl], start=st_, stop=sp_),
                               tS if (sp_ and sbk == banks[1]) else None)
                        if sp_:
                            marks[sbk] = None
                marks[2 * pair] = m
                marks[2 * pair + 1] = m

        def evac_rope(dstF, marks, prev_rope=None):
            wait("scalar", vS, prev_rope)
            av = None
            for sbk in range(NSB):
                sl = slice(512 * sbk, 512 * sbk + 512)
                wait("scalar", tS, marks[sbk])
                av = op("scalar", lambda e, sl=sl, sbk=sbk: e.activation(QR[:, sl], P[sbk][:], AF.Copy), aS)
                bank_free_v[sbk] = (aS, av)
            wait("gpsimd", aS, av)
            dma("gpsimd", iog, QSW[0:64, :], QR[64:128, :])
            dsw = dma("gpsimd", iog, QSW[64:128, :], QR[0:64, :])
            wait("vector", aS, av)
            op("vector", lambda e, dstF=dstF: e.tensor_mul(dstF[:], QR[:], TA[:]))
            wait("vector", iog, dsw)
            op("vector", lambda e: e.tensor_mul(QR[:], QSW[:], TB[:]))
            return op("vector", lambda e, dstF=dstF: e.tensor_add(dstF[:], dstF[:], QR[:]), vS)

        vh_gate = [None]

        def emit_pv(g):
            tc, slot, ntc, is_diag, st_, sp_ = chunk_info[g]
            if is_diag:
                wait("tensor", vS, vm_marks[g])
            else:
                wait("tensor", aS, exp_marks[g])
            if tc == 8:
                wait("tensor", vS, vh_gate[0])
            if st_:
                wait("tensor", vS, v_ah_last)
            op("tensor", lambda e, tc=tc, slot=slot, st_=st_, sp_=sp_:
               e.matmul(PPV[:], VH[:, tc, :], PT[:, slot, :], start=st_, stop=sp_))
            pv_marks[g] = op("tensor", lambda e, slot=slot, st_=st_, sp_=sp_:
                             e.matmul(PL[0:1, :], onescol[:], PT[:, slot, :],
                                      start=st_, stop=sp_), tS)

        def emit_bc(pend):
            nonlocal v_ah_last
            a_rr_p, qb_p, h_p = pend
            wait("tensor", aS, a_rr_p)
            t_bc = op("tensor", lambda e: e.matmul(PS4[:], ones1[:], LROW[:],
                                                   start=True, stop=True), tS)
            wait("vector", tS, t_bc)
            op("vector", lambda e: e.tensor_copy(RRB[:, 0, :], PS4[:]))
            qsl_p = slice(QS * qb_p, QS * qb_p + QS)
            v_ah_last = op("vector", lambda e, h_p=h_p, qsl_p=qsl_p:
                           e.tensor_mul(AH[:, h_p, qsl_p], PPV[:], RRB[:, 0, :]), vS)

        wait("tensor", vS, v_apply)
        for sbk in range(NSB):
            bank_free_v[sbk] = (vS, v_apply)

        for h in range(NH):
            tq = [None] * NSB
            proj(WQB, tq, ld_w[h][0])
            v_qf = evac_rope(QF, tq)
            if h + 1 < NH:
                wait("gpsimd", tS, tq[NSB - 1])
                ld_w[h + 1][0] = dma("gpsimd", iog, WQB[:], wq[h + 1])
            if h < NCH:
                emit_ffn_stat(h)
            tk = [None] * NSB
            proj(WKB, tk, ld_w[h][1])
            v_kf = evac_rope(KF, tk, prev_rope=v_qf)
            if h + 1 < NH:
                wait("gpsimd", tS, tk[NSB - 1])
                ld_w[h + 1][1] = dma("gpsimd", iog, WKB[:], wk[h + 1])
            wait("tensor", iog, ld_w[h][2])
            v_vh = None
            v_vt = None
            t_tr = None
            for half in range(2):
                banks = (2 * half, 2 * half + 1)
                for sbk in banks:
                    wait_bf("tensor", bank_free_v[sbk])
                    wait("tensor", aS, bank_exp_a[sbk])
                    bank_free_v[sbk] = None
                    bank_exp_a[sbk] = None
                if half == 1:
                    wait_bf("tensor", v_vt)
                    for j in range(8):
                        t_tr = op("tensor", lambda e, j=j:
                                  e.transpose(PS5B[:, 128 * j:128 * j + 128],
                                              VT[:, 128 * j:128 * j + 128], ident[:]),
                                  tS if j == 7 else None)
                t_vm = None
                for ic in range(NCH):
                    st_, sp_ = (ic == 0), (ic == NCH - 1)
                    for sbk in banks:
                        sl = slice(512 * sbk, 512 * sbk + 512)
                        t_vm = op("tensor",
                                  lambda e, ic=ic, sl=sl, sbk=sbk, st_=st_, sp_=sp_:
                                  e.matmul(P[sbk][:], WVB[:, 128 * ic:128 * ic + 128],
                                           XN[:, ic, sl], start=st_, stop=sp_),
                                  tS if (sp_ and sbk == banks[1]) else None)
                if half == 0:
                    wait("scalar", vS, v_kf)
                    for j in range(2):
                        sbk = banks[j]
                        sl1 = slice(512 * j, 512 * j + 512)
                        wait("scalar", tS, t_vm)
                        v_vt = op("scalar", lambda e, sl1=sl1, sbk=sbk:
                                  e.activation(VT[:, sl1], P[sbk][:], AF.Copy), aS)
                        bank_free_v[sbk] = (aS, v_vt)
                        v_vt = (aS, v_vt)
                else:
                    wait("scalar", tS, t_tr)
                    v_vh = op("scalar", lambda e:
                              e.activation(VH[:, 0:8, :], PS5B[:], AF.Copy), aS)
                    v_vh = (aS, v_vh)
                    for j in range(2):
                        sbk = banks[j]
                        sl1 = slice(512 * j, 512 * j + 512)
                        wait("scalar", tS, t_vm)
                        v_vt = op("scalar", lambda e, sl1=sl1, sbk=sbk:
                                  e.activation(VT[:, sl1], P[sbk][:], AF.Copy), aS)
                        bank_free_v[sbk] = (aS, v_vt)
                        v_vt = (aS, v_vt)
            if h + 1 < NH:
                wait("gpsimd", tS, t_vm)
                ld_w[h + 1][2] = dma("gpsimd", iog, WVB[:], wv[h + 1])
            pend_tr = (v_vt, v_vh)

            vh_gate[0] = None
            wait("tensor", vS, v_qf)
            wait("tensor", vS, v_kf)
            for qb in range(NQB):
                qsl = slice(QS * qb, QS * qb + QS)
                ntc = 4 * qb + 4
                for tc in range(ntc):
                    g = g_ctr
                    g_ctr += 1
                    bank = g % 2
                    wait("tensor", aS, exp_marks.get(g - 2))
                    wait_bf("tensor", bank_free_v[bank])
                    bank_free_v[bank] = None
                    t_sc = op("tensor", lambda e, tc=tc, qsl=qsl, bank=bank:
                              e.matmul(P[bank][:], KF[:, 128 * tc:128 * tc + 128], QF[:, qsl],
                                       start=True, stop=True), tS)
                    slot = g % 4
                    j = tc - 4 * qb
                    wait("scalar", tS, t_sc)
                    wait("scalar", tS, pv_marks.get(g - 4))
                    is_diag = j >= 0
                    if j > 0:
                        op("scalar", lambda e, slot=slot, j=j: e.memzero(PT[:, slot, 0:128 * j]))
                    if is_diag:
                        px = g % 2
                        wait("scalar", vS, vm_marks.get(g - 2))
                        if j < 3:
                            op("scalar", lambda e, px=px, j=j, bank=bank:
                               e.activation(PTX[:, px, :], P[bank][:, 128 * j:128 * j + 128],
                                            AF.Exp, bias=NEGC[:], scale=SCALE))
                            a_e = op("scalar", lambda e, slot=slot, j=j, bank=bank:
                                     e.activation(PT[:, slot, 128 * (j + 1):QS],
                                                  P[bank][:, 128 * (j + 1):QS],
                                                  AF.Exp, bias=NEGC[:], scale=SCALE), aS)
                        else:
                            a_e = op("scalar", lambda e, px=px, j=j, bank=bank:
                                     e.activation(PTX[:, px, :], P[bank][:, 128 * j:128 * j + 128],
                                                  AF.Exp, bias=NEGC[:], scale=SCALE), aS)
                        wait("vector", aS, a_e)
                        vm_marks[g] = op("vector", lambda e, slot=slot, j=j, px=px:
                                         e.tensor_mul(PT[:, slot, 128 * j:128 * j + 128],
                                                      PTX[:, px, :], tri[:]), vS)
                    else:
                        a_e = op("scalar", lambda e, slot=slot, bank=bank:
                                 e.activation(PT[:, slot, :], P[bank][:], AF.Exp,
                                              bias=NEGC[:], scale=SCALE), aS)
                    exp_marks[g] = a_e
                    bank_exp_a[bank] = a_e
                    chunk_info[g] = (tc, slot, ntc, is_diag, tc == 0, tc == ntc - 1)
                    if tc == 0 and t_bc_pending is not None:
                        emit_bc(t_bc_pending)
                        t_bc_pending = None
                    if qb == 0 and tc == 1 and pend_tr is not None:
                        v_vt_p, v_vh_p = pend_tr
                        pend_tr = None
                        wait_bf("tensor", v_vt_p)
                        wait_bf("tensor", v_vh_p)
                        for jj in range(8):
                            t_tr2 = op("tensor", lambda e, jj=jj:
                                       e.transpose(PS5B[:, 128 * jj:128 * jj + 128],
                                                   VT[:, 128 * jj:128 * jj + 128], ident[:]),
                                       tS if jj == 7 else None)
                        wait("vector", tS, t_tr2)
                        v_vh2 = op("vector", lambda e:
                                   e.tensor_copy(VH[:, 8:16, :], PS5B[:]), vS)
                        vh_gate[0] = v_vh2
                    if tc >= 2:
                        emit_pv(g - 2)
                if ntc >= 2:
                    emit_pv(g_ctr - 2)
                emit_pv(g_ctr - 1)
                wait("scalar", tS, pv_marks[g_ctr - 1])
                op("scalar", lambda e: e.activation(LROW[:], PL[0:1, :], AF.Ln))
                a_rr = op("scalar", lambda e: e.activation(LROW[:], LROW[:], AF.Exp, scale=-1.0), aS)
                t_bc_pending = (a_rr, qb, h)

            if DEBUG_OUT == "h0" and h == 0:
                wait("gpsimd", vS, max(v_qf, v_kf))
                wait("gpsimd", aS, aS.n)
                dma("gpsimd", iog, dbg[:, 0, :], QF[:])
                dma("gpsimd", iog, dbg[:, 1, :], KF[:])
                dma("gpsimd", iog, dbg[:, 2, :], VH[:])

        a_rr_p = t_bc_pending[0]
        emit_bc(t_bc_pending)
        t_bc_pending = None

        wait("tensor", gS, ffn_acc_v[0])
        wait("tensor", aS, a_rr_p)
        t_s = op("tensor", lambda e: e.matmul(PL[0:1, 0:1], ACC[:], onescolf[:], start=True, stop=True), tS)
        wait("scalar", tS, t_s)
        a_s = op("scalar", lambda e: e.activation(SVAL[:, 0:1], PL[0:1, 0:1], AF.Copy,
                                                  scale=1.0 / float(H * H)), aS)
        op("scalar", lambda e: e.activation(SVAL[:, 4:5], SVAL[:, 0:1], AF.Ln))
        a_i2s = op("scalar", lambda e: e.activation(SVAL[:, 2:3], SVAL[:, 4:5], AF.Exp,
                                                    bias=LN2V[:], scale=-1.0), aS)
        wait("vector", aS, a_i2s)
        op("vector", lambda e: e.tensor_scalar(SVAL[:, 1:2], SVAL[:, 0:1], 0.5, 0.5 * EPS_FFN, AL.mult, AL.add))
        v_sv = op("vector", lambda e: e.tensor_scalar(SVAL[:, 3:4], SVAL[:, 0:1], 0.5, None, AL.mult), vS)
        wait("tensor", vS, v_sv)
        op("tensor", lambda e: e.matmul(PL[:, 0:1], ones1[:], SVAL[0:1, 1:2], start=True, stop=True))
        op("tensor", lambda e: e.matmul(PL[:, 1:2], ones1[:], SVAL[0:1, 2:3], start=True, stop=True))
        t_b3 = op("tensor", lambda e: e.matmul(PL[:, 2:3], ones1[:], SVAL[0:1, 3:4], start=True, stop=True), tS)
        wait("vector", tS, t_b3)
        op("vector", lambda e: e.tensor_copy(TPOS[:], PL[:, 0:1]))
        op("vector", lambda e: e.tensor_scalar(TNEG[:], PL[:, 0:1], -1.0, None, AL.mult))
        op("vector", lambda e: e.tensor_copy(I2S[:], PL[:, 1:2]))
        v_cst = op("vector", lambda e: e.tensor_copy(S2K[:], PL[:, 2:3]), vS)

        ld_wo = [None] * NCH
        ld_wf = [None] * NCH
        ld_x = [None] * NCH
        wait("gpsimd", vS, v_ah_last)
        ld_wo[0] = dma("gpsimd", iog, WVB[:], wo[0])
        wait("sync", aS, ffn_abs[NCH - 1])
        ld_wf[0] = dma("sync", io, WS[:], wf[0])
        ld_wf[1] = dma("sync", io, WS2[:], wf[1])
        wait("scalar", vS, v_apply)
        ld_x[0] = dma("scalar", ios, RB[:], xt[0])

        woslab = [WQB, QR]
        w2slab = [WKB, QSW]
        sgslab = [QF, KF]
        t_pf = [t_s, t_s]
        v_w2 = [None] * NCH
        v_wo = [None] * NCH
        v_w2_prev = [None]
        ob_store = [None, None]
        v_ob = [None, None]

        def final_prep(ot):
            par = ot % 2
            wait("vector", iog, ld_wo[ot])
            wait("vector", tS, t_pf[par])
            v_wo[ot] = op("vector", lambda e, par=par:
                          e.tensor_scalar(woslab[par][:], WVB[:], I2S[:], None, AL.mult), vS)
            if ot + 1 < NCH:
                wait("gpsimd", vS, v_wo[ot])
                ld_wo[ot + 1] = dma("gpsimd", iog, WVB[:], wo[ot + 1])
            wait("scalar", io, ld_wf[ot])
            wait("scalar", vS, v_w2_prev[0] if v_w2_prev[0] is not None else v_cst)
            op("scalar", lambda e, ot=ot: e.activation(QF[:], xstage[ot % 2][:], AF.Sign, bias=TNEG[:]))
            a_sg = op("scalar", lambda e, ot=ot: e.activation(KF[:], xstage[ot % 2][:], AF.Sign, bias=TPOS[:]), aS)
            if ot + 2 < NCH:
                wait("sync", aS, a_sg)
                ld_wf[ot + 2] = dma("sync", io, xstage[ot % 2][:], wf[ot + 2])
            wait("vector", aS, a_sg)
            wait("vector", tS, t_pf[par])
            v_w2[ot] = op("vector", lambda e, par=par: e.tensor_add(w2slab[par][:], QF[:], KF[:]), vS)
            v_w2_prev[0] = v_w2[ot]

        final_prep(0)
        for ot in range(NCH):
            par = ot % 2
            if ot + 1 < NCH:
                final_prep(ot + 1)
            wait("tensor", vS, v_w2[ot])
            wait("tensor", vS, v_wo[ot])
            for pair in range(2):
                banks = (2 * pair, 2 * pair + 1)
                for sbk in banks:
                    wait_bf("tensor", bank_free_v[sbk])
                    wait("tensor", aS, bank_exp_a[sbk])
                    bank_free_v[sbk] = None
                    bank_exp_a[sbk] = None
                for ic in range(NCH):
                    for sbk in banks:
                        sl = slice(512 * sbk, 512 * sbk + 512)
                        op("tensor", lambda e, ic=ic, sl=sl, sbk=sbk, par=par:
                           e.matmul(P[sbk][:], woslab[par][:, 128 * ic:128 * ic + 128],
                                    AH[:, ic, sl], start=(ic == 0), stop=False))
                t_f = None
                for ic in range(NCH):
                    sp_ = (ic == NCH - 1)
                    for sbk in banks:
                        sl = slice(512 * sbk, 512 * sbk + 512)
                        t_f = op("tensor", lambda e, ic=ic, sl=sl, sbk=sbk, par=par, sp_=sp_:
                                 e.matmul(P[sbk][:], w2slab[par][:, 128 * ic:128 * ic + 128],
                                          XN[:, ic, sl], start=False, stop=sp_),
                                 tS if (sp_ and sbk == banks[1]) else None)
                t_pf[par] = t_f
                for sbk in banks:
                    sl = slice(512 * sbk, 512 * sbk + 512)
                    ob = sbk % 2
                    wait("vector", tS, t_f)
                    wait("vector", io, ob_store[ob])
                    wait("vector", ios, ld_x[ot])
                    wait("vector", io, b_bias)
                    op("vector", lambda e, sbk=sbk, ot=ot, ob=ob:
                       e.tensor_scalar(RRB[:, ob, :], P[sbk][:], S2K[:], bias_sb[:, ot:ot + 1],
                                       AL.mult, AL.add))
                    v_ob[ob] = op("vector", lambda e, sl=sl, ob=ob:
                                  e.tensor_add(RRB[:, ob, :], RRB[:, ob, :], RB[:, sl]), vS)
                    bank_free_v[sbk] = (vS, v_ob[ob])
                    wait("sync", vS, v_ob[ob])
                    ob_store[ob] = dma("sync", io, yt[ot][:, sl], RRB[:, ob, :])
            if ot + 1 < NCH:
                wait("scalar", vS, v_ob[1])
                ld_x[ot + 1] = dma("scalar", ios, RB[:], xt[ot + 1])

        if DEBUG_OUT == "xn":
            wait("gpsimd", vS, vS.n)
            for ic in range(NCH):
                dma("gpsimd", iog, dbg[:, ic, :], XN[:, ic, :])
        elif DEBUG_OUT == "ah":
            wait("gpsimd", vS, vS.n)
            for ic in range(NCH):
                dma("gpsimd", iog, dbg[:, ic, :], AH[:, ic, :])

        wait("sync", io, io.n)
        wait("gpsimd", iog, iog.n)
        wait("scalar", ios, ios.n)

        block = st.enter_context(nc.Block())

        def replay(which):
            def f(e):
                for eng, fn in events:
                    if eng == which:
                        fn(e)
            return f

        block.sync(replay("sync"))
        block.gpsimd(replay("gpsimd"))
        block.vector(replay("vector"))
        block.scalar(replay("scalar"))
        block.tensor(replay("tensor"))

    nc.finalize()
    return nc


_NC_CACHE = None


def kernel(L0_attn_x, L0_attn_Wq, L0_attn_Wk, L0_attn_Wv, L0_attn_Wo,
           L0_ffn_weight, L0_ffn_bias):
    global _NC_CACHE
    x = np.asarray(L0_attn_x)
    B = x.shape[0]

    def wtile(w):
        w = np.asarray(w)
        return np.ascontiguousarray(w.reshape(NH, 128, NCH, 128).transpose(0, 3, 2, 1))

    wqh = wtile(L0_attn_Wq)
    wkh = wtile(L0_attn_Wk)
    wvh = wtile(L0_attn_Wv)
    woh = wtile(L0_attn_Wo)
    wfh = wtile(L0_ffn_weight)
    bias_h = np.ascontiguousarray(np.asarray(L0_ffn_bias).reshape(NCH, 128).T)

    if _NC_CACHE is None:
        _NC_CACHE = build_program()
    nc = _NC_CACHE

    in_maps = []
    for b in range(B):
        xtb = np.ascontiguousarray(x[b].T).reshape(NCH, 128, S)
        in_maps.append({"xt": xtb, "wq": wqh, "wk": wkh, "wv": wvh,
                        "wo": woh, "wf": wfh, "bias_d": bias_h})

    res = run_bass_kernel_spmd(nc, in_maps, core_ids=list(range(B)))
    outs = []
    for b in range(B):
        ytb = np.asarray(res.results[b]["yt"])
        outs.append(ytb.transpose(2, 0, 1).reshape(S, H))
    return np.stack(outs, axis=0)


# version 10
# speedup vs baseline: 1.0296x; 1.0296x over previous
import math
from contextlib import ExitStack

import numpy as np
import ml_dtypes
import concourse.bass as bass
import concourse.mybir as mybir
from concourse.bass_utils import run_bass_kernel_spmd

F32 = mybir.dt.float32
BF16 = mybir.dt.bfloat16
I32 = mybir.dt.int32
AL = mybir.AluOpType
AF = mybir.ActivationFunctionType

S = 2048
H = 2048
NH = 16
HD = 128
NCH = 16
NQB = 4
QS = 512
NSB = 4
SCALE = 1.0 / math.sqrt(HD)
CSHIFT = -20.0
EPS_LN = 1e-5
EPS_FFN = 1e-10
PI = math.pi
TWO_PI = 2.0 * math.pi
_C1 = float(np.float32(6.28125))
_C2 = float(np.float32(np.float64(TWO_PI) - np.float64(_C1)))
_C3 = float(np.float32(np.float64(TWO_PI) - np.float64(_C1) - np.float64(_C2)))
LN1E4_64 = math.log(10000.0) / 64.0

DEBUG_OUT = None


class Sem:
    def __init__(self, handle):
        self.h = handle
        self.n = 0

    def inc(self, k=1):
        self.n += k
        return self.n


def build_program():
    nc = bass.Bass("TRN2", target_bir_lowering=False)

    xt = nc.dram_tensor("xt", [NCH, 128, S], F32, kind="ExternalInput")
    wq = nc.dram_tensor("wq", [NH, 128, NCH, 128], F32, kind="ExternalInput")
    wk = nc.dram_tensor("wk", [NH, 128, NCH, 128], F32, kind="ExternalInput")
    wv = nc.dram_tensor("wv", [NH, 128, NCH, 128], F32, kind="ExternalInput")
    wo = nc.dram_tensor("wo", [NCH, 128, NCH, 128], F32, kind="ExternalInput")
    wf = nc.dram_tensor("wf", [NCH, 128, NCH, 128], F32, kind="ExternalInput")
    bias_d = nc.dram_tensor("bias_d", [128, NCH], F32, kind="ExternalInput")
    yt = nc.dram_tensor("yt", [NCH, 128, S], F32, kind="ExternalOutput")
    dbg = None
    if DEBUG_OUT is not None:
        dbg = nc.dram_tensor("dbg", [128, NCH, S], F32, kind="ExternalOutput")

    events = []

    def ev(eng, fn):
        events.append((eng, fn))

    def wait(eng, sem, v):
        if v is None or v <= 0:
            return
        assert v <= sem.n, f"wait on future value {v} > {sem.n}"
        ev(eng, lambda e, h=sem.h, vv=v: e.wait_ge(h, vv))

    def dma(eng, sem, dst, src, slow=False):
        v = sem.inc(16)
        ev(eng, lambda e, d=dst, s_=src, h=sem.h, sl_=slow:
           e.dma_start(d, s_, allow_slow_non_contiguous=sl_).then_inc(h, 16))
        return v

    def op(eng, fn, sem=None):
        if sem is None:
            ev(eng, fn)
            return None
        v = sem.inc(1)
        ev(eng, lambda e, f=fn, h=sem.h: f(e).then_inc(h, 1))
        return v

    with ExitStack() as st:
        def sb(name, shape, dt):
            return st.enter_context(nc.sbuf_tensor(name, shape, dt))

        def ps(name, shape, dt=F32):
            return st.enter_context(nc.psum_tensor(name, shape, dt))

        XN = sb("XN", [128, NCH, S], BF16)
        AH = sb("AH", [128, NCH, S], BF16)
        WS = sb("WS", [128, S], F32)
        WS2 = sb("WS2", [128, S], F32)
        RB = sb("RB", [128, S], F32)
        WQB = sb("WQB", [128, S], BF16)
        WKB = sb("WKB", [128, S], BF16)
        WVB = sb("WVB", [128, S], BF16)
        QR = sb("QR", [128, S], BF16)
        QSW = sb("QSW", [128, S], BF16)
        QF = sb("QF", [128, S], BF16)
        KF = sb("KF", [128, S], BF16)
        VH = sb("VH", [128, NCH, 128], BF16)
        PT = sb("PT", [128, 4, QS], BF16)
        PTX = sb("PTX", [128, 2, 128], BF16)
        RRB = sb("RRB", [128, 2, QS], F32)
        KI = sb("KI", [128, 256], I32)
        LROW = sb("LROW", [1, QS], F32)
        TA = sb("TA", [128, S], BF16)
        TB = sb("TB", [128, S], BF16)
        onescol = sb("onescol", [128, 1], BF16)
        onescolf = sb("onescolf", [128, 1], F32)
        ones1 = sb("ones1", [1, 128], F32)
        tri = sb("tri", [128, 128], BF16)
        ident = sb("ident", [128, 128], BF16)
        bias_sb = sb("bias_sb", [128, NCH], F32)
        J64 = sb("J64", [128, 1], F32)
        INV = sb("INV", [128, 1], F32)
        SGN = sb("SGN", [128, 1], F32)
        NEGC = sb("NEGC", [128, 1], F32)
        EPSV = sb("EPSV", [1, 1], F32)
        LN2V = sb("LN2V", [1, 1], F32)
        ACC = sb("ACC", [128, 1], F32)
        ACC2 = sb("ACC2", [128, 1], F32)
        ACC3 = sb("ACC3", [128, 1], F32)
        SVAL = sb("SVAL", [1, 8], F32)
        TPOS = sb("TPOS", [128, 1], F32)
        TNEG = sb("TNEG", [128, 1], F32)
        S2K = sb("S2K", [128, 1], F32)
        I2S = sb("I2S", [128, 1], F32)

        P = [ps(f"P{i}", [128, 512]) for i in range(4)]
        PS4 = ps("PS4", [128, 512])
        PS5B = ps("PS5B", [128, 1024], BF16)
        PPV = ps("PPV", [128, 512])
        PL = ps("PL", [128, 512])

        io = Sem(st.enter_context(nc.semaphore("io")))
        iog = Sem(st.enter_context(nc.semaphore("iog")))
        ios = Sem(st.enter_context(nc.semaphore("ios")))
        tS = Sem(st.enter_context(nc.semaphore("tS")))
        vS = Sem(st.enter_context(nc.semaphore("vS")))
        aS = Sem(st.enter_context(nc.semaphore("aS")))
        gS = Sem(st.enter_context(nc.semaphore("gS")))

        VT = QSW
        xstage = [WS, WS2]

        def gp_setup(e):
            e.iota(WS2[:], [[1, S]], channel_multiplier=0, allow_small_or_imprecise_dtypes=True)
            e.iota(J64[0:64, :], [[0, 1]], channel_multiplier=1, allow_small_or_imprecise_dtypes=True)
            e.iota(J64[64:128, :], [[0, 1]], channel_multiplier=1, allow_small_or_imprecise_dtypes=True)
            e.memset(SGN[0:64, :], -1.0)
            e.memset(SGN[64:128, :], 1.0)
            e.memset(onescol[:], 1.0)
            e.memset(onescolf[:], 1.0)
            e.memset(ones1[:], 1.0)
            e.memset(NEGC[:], CSHIFT)
            e.memset(EPSV[:], EPS_LN)
            e.memset(LN2V[:], math.log(2.0))
            e.memset(QF[:, 0:128], 1.0)
            e.affine_select(tri[:], QF[:, 0:128], [[1, 128]], AL.is_ge, 0.0,
                            base=0, channel_multiplier=-1)
            return e.affine_select(ident[:], QF[:, 0:128], [[1, 128]], AL.is_equal, 0.0,
                                   base=0, channel_multiplier=-1)
        g_setup = op("gpsimd", gp_setup, gS)
        b_bias = dma("sync", io, bias_sb[:], bias_d[:])

        ld_w = [[None, None, None] for _ in range(NH)]
        wait("gpsimd", gS, g_setup)
        ld_w[0][0] = dma("gpsimd", iog, WQB[:], wq[0])
        ld_w[0][1] = dma("gpsimd", iog, WKB[:], wk[0])
        ld_w[0][2] = dma("gpsimd", iog, WVB[:], wv[0])

        wait("scalar", gS, g_setup)
        a_inv = op("scalar", lambda e: e.activation(INV[:], J64[:], AF.Exp, scale=-LN1E4_64), aS)
        wait("vector", gS, g_setup)
        wait("vector", aS, a_inv)
        op("vector", lambda e: e.tensor_scalar(RB[:], WS2[:], INV[:], None, AL.mult))
        for cc in range(8):
            sl = slice(256 * cc, 256 * cc + 256)
            op("vector", lambda e, sl=sl: e.tensor_scalar(WS2[:, sl], RB[:, sl], 1.0 / TWO_PI, None, AL.mult))
            op("vector", lambda e, sl=sl: e.tensor_copy(KI[:], WS2[:, sl]))
            op("vector", lambda e, sl=sl: e.tensor_copy(WS2[:, sl], KI[:]))
        op("vector", lambda e: e.scalar_tensor_tensor(WS[:], WS2[:], _C1, RB[:], AL.mult, AL.subtract))
        op("vector", lambda e: e.scalar_tensor_tensor(RB[:], WS2[:], _C2, WS[:], AL.mult, AL.add))
        op("vector", lambda e: e.scalar_tensor_tensor(WS[:], WS2[:], _C3, RB[:], AL.mult, AL.add))
        op("vector", lambda e: e.tensor_scalar(WS2[:], WS[:], -PI, TWO_PI, AL.is_lt, AL.mult))
        op("vector", lambda e: e.tensor_add(RB[:], WS[:], WS2[:]))
        op("vector", lambda e: e.tensor_scalar(WS2[:], RB[:], PI, TWO_PI, AL.is_gt, AL.mult))
        v_rs = op("vector", lambda e: e.tensor_sub(WS[:], RB[:], WS2[:]), vS)
        wait("scalar", vS, v_rs)
        a_sin = op("scalar", lambda e: e.activation(TA[:], WS[:], AF.Sin, scale=-1.0), aS)
        wait("vector", aS, a_sin)
        op("vector", lambda e: e.tensor_scalar(RB[:], WS[:], PI / 2.0, None, AL.add))
        op("vector", lambda e: e.tensor_scalar(WS2[:], RB[:], PI, TWO_PI, AL.is_gt, AL.mult))
        v_rc = op("vector", lambda e: e.tensor_sub(WS[:], RB[:], WS2[:]), vS)
        wait("scalar", vS, v_rc)
        a_cos = op("scalar", lambda e: e.activation(QR[:], WS[:], AF.Sin), aS)
        wait("vector", aS, a_cos)
        v_tab = op("vector", lambda e: e.tensor_scalar(TB[:], QR[:], SGN[:], None, AL.mult), vS)

        d_x = [None] * NCH
        v_cast = [None] * NCH
        wait("sync", vS, v_tab)
        d_x[0] = dma("sync", io, WS[:], xt[0])
        d_x[1] = dma("sync", io, WS2[:], xt[1])
        for ic in range(NCH):
            wait("vector", io, d_x[ic])
            v_cast[ic] = op("vector", lambda e, ic=ic: e.tensor_copy(AH[:, ic, :], xstage[ic % 2][:]), vS)
            if ic + 2 < NCH:
                wait("sync", vS, v_cast[ic])
                d_x[ic + 2] = dma("sync", io, xstage[ic % 2][:], xt[ic + 2])
        v_ep = None
        for sbk in range(NSB):
            sl = slice(512 * sbk, 512 * sbk + 512)
            sq_marks = [None, None]
            for ic in range(NCH):
                wait("scalar", tS, sq_marks[ic % 2])
                if ic == 0:
                    wait("scalar", vS, v_cast[NCH - 1])
                    if v_ep is not None:
                        wait("scalar", vS, v_ep)
                a_sq = op("scalar", lambda e, ic=ic, sl=sl:
                          e.activation(PT[:, ic % 2, :], AH[:, ic, sl], AF.Square), aS)
                wait("tensor", aS, a_sq)
                st_, sp_ = (ic == 0), (ic == NCH - 1)
                op("tensor", lambda e, ic=ic, sl=sl, st_=st_, sp_=sp_:
                   e.matmul(PS4[0:1, :], onescol[:], AH[:, ic, sl], start=st_, stop=sp_))
                sq_marks[ic % 2] = op("tensor", lambda e, ic=ic, st_=st_, sp_=sp_:
                                      e.matmul(PL[0:1, :], onescol[:], PT[:, ic % 2, :],
                                               start=st_, stop=sp_), tS)
            t_mm = sq_marks[(NCH - 1) % 2]
            wait("vector", tS, t_mm)
            op("vector", lambda e: e.tensor_scalar(LROW[:], PS4[0:1, :], 1.0 / H, None, AL.mult))
            op("vector", lambda e: e.tensor_scalar(RRB[0:1, 0, :], PL[0:1, :], 1.0 / H, None, AL.mult))
            op("vector", lambda e: e.tensor_mul(PS4[0:1, :], LROW[:], LROW[:]))
            v_var = op("vector", lambda e: e.tensor_sub(RRB[0:1, 0, :], RRB[0:1, 0, :], PS4[0:1, :]), vS)
            wait("scalar", vS, v_var)
            op("scalar", lambda e: e.activation(RRB[0:1, 0, :], RRB[0:1, 0, :], AF.Ln, bias=EPSV[:]))
            a_r = op("scalar", lambda e: e.activation(RRB[0:1, 0, :], RRB[0:1, 0, :], AF.Exp, scale=-0.5), aS)
            wait("tensor", aS, a_r)
            op("tensor", lambda e, sbk=sbk: e.matmul(P[sbk][:], ones1[:], LROW[:], start=True, stop=True))
            t_r = op("tensor", lambda e: e.matmul(PS4[:], ones1[:], RRB[0:1, 0, :], start=True, stop=True), tS)
            wait("vector", tS, t_r)
            v_ep = op("vector", lambda e, sl=sl: e.tensor_copy(RB[:, sl], PS4[:]), vS)
            wait("tensor", vS, v_ep)

        d_x = [None] * NCH
        wait("sync", vS, v_ep)
        d_x[0] = dma("sync", io, WS[:], xt[0])
        d_x[1] = dma("sync", io, WS2[:], xt[1])
        v_xn = [None] * NCH
        for ic in range(NCH):
            wait("vector", io, d_x[ic])
            for sbk in range(NSB):
                sl = slice(512 * sbk, 512 * sbk + 512)
                op("vector", lambda e, ic=ic, sl=sl, sbk=sbk:
                   e.tensor_sub(RRB[:, 1, :], xstage[ic % 2][:, sl], P[sbk][:]))
                v_xn[ic] = op("vector", lambda e, ic=ic, sl=sl:
                              e.tensor_mul(XN[:, ic, sl], RRB[:, 1, :], RB[:, sl]), vS)
            if ic + 2 < NCH:
                wait("sync", vS, v_xn[ic])
                d_x[ic + 2] = dma("sync", io, xstage[ic % 2][:], xt[ic + 2])
        v_apply = v_xn[NCH - 1]

        ffn_d = [None] * NCH
        ffn_abs = [None] * NCH
        ffn_acc_v = [None]

        def emit_ffn_stat(ot):
            if ot < 2:
                wait("sync", vS, v_apply)
            else:
                wait("sync", aS, ffn_abs[ot - 2])
            ffn_d[ot] = dma("sync", io, xstage[ot % 2][:], wf[ot])
            wait("scalar", io, ffn_d[ot])
            for pc in range(4):
                psl = slice(512 * pc, 512 * pc + 512)
                first = (ot == 0 and pc == 0)
                acc_t = ACC if first else (ACC2 if pc % 2 == 0 else ACC3)
                if not first:
                    wait("scalar", gS, ffn_acc_v[0])
                ffn_abs[ot] = op("scalar", lambda e, ot=ot, psl=psl, acc_t=acc_t:
                                 e.activation(RRB[:, 1, :], xstage[ot % 2][:, psl], AF.Abs,
                                              accum_out=acc_t[:]), aS)
                if not first:
                    wait("gpsimd", aS, ffn_abs[ot])
                    ffn_acc_v[0] = op("gpsimd", lambda e, acc_t=acc_t:
                                      e.tensor_add(ACC[:], ACC[:], acc_t[:]), gS)

        bank_free_v = [None] * 4
        bank_exp_a = [None] * 4

        def wait_bf(eng, bf):
            if bf is not None:
                wait(eng, bf[0], bf[1])
        exp_marks = {}
        pv_marks = {}
        vm_marks = {}
        chunk_info = {}
        g_ctr = 0
        v_ah_last = None
        t_bc_pending = None

        def proj(wbuf, marks, ldmark):
            wait("tensor", iog, ldmark)
            for pair in range(2):
                banks = (2 * pair, 2 * pair + 1)
                for sbk in banks:
                    wait_bf("tensor", bank_free_v[sbk])
                    wait("tensor", aS, bank_exp_a[sbk])
                    bank_free_v[sbk] = None
                    bank_exp_a[sbk] = None
                for ic in range(NCH):
                    st_, sp_ = (ic == 0), (ic == NCH - 1)
                    for sbk in banks:
                        sl = slice(512 * sbk, 512 * sbk + 512)
                        m = op("tensor",
                               lambda e, wbuf=wbuf, ic=ic, sl=sl, sbk=sbk, st_=st_, sp_=sp_:
                               e.matmul(P[sbk][:], wbuf[:, 128 * ic:128 * ic + 128],
                                        XN[:, ic, sl], start=st_, stop=sp_),
                               tS if (sp_ and sbk == banks[1]) else None)
                        if sp_:
                            marks[sbk] = None
                marks[2 * pair] = m
                marks[2 * pair + 1] = m

        def evac_rope(dstF, marks, prev_rope=None):
            wait("scalar", vS, prev_rope)
            av = None
            for sbk in range(NSB):
                sl = slice(512 * sbk, 512 * sbk + 512)
                wait("scalar", tS, marks[sbk])
                av = op("scalar", lambda e, sl=sl, sbk=sbk: e.activation(QR[:, sl], P[sbk][:], AF.Copy), aS)
                bank_free_v[sbk] = (aS, av)
            wait("gpsimd", aS, av)
            dma("gpsimd", iog, QSW[0:64, :], QR[64:128, :])
            dsw = dma("gpsimd", iog, QSW[64:128, :], QR[0:64, :])
            wait("vector", aS, av)
            op("vector", lambda e, dstF=dstF: e.tensor_mul(dstF[:], QR[:], TA[:]))
            wait("vector", iog, dsw)
            op("vector", lambda e: e.tensor_mul(QR[:], QSW[:], TB[:]))
            return op("vector", lambda e, dstF=dstF: e.tensor_add(dstF[:], dstF[:], QR[:]), vS)

        vh_gate = [None]

        def emit_pv(g):
            tc, slot, ntc, is_diag, st_, sp_ = chunk_info[g]
            if is_diag:
                wait("tensor", vS, vm_marks[g])
            else:
                wait("tensor", aS, exp_marks[g])
            if tc == 8:
                wait("tensor", vS, vh_gate[0])
            if st_:
                wait("tensor", vS, v_ah_last)
            op("tensor", lambda e, tc=tc, slot=slot, st_=st_, sp_=sp_:
               e.matmul(PPV[:], VH[:, tc, :], PT[:, slot, :], start=st_, stop=sp_))
            pv_marks[g] = op("tensor", lambda e, slot=slot, st_=st_, sp_=sp_:
                             e.matmul(PL[0:1, :], onescol[:], PT[:, slot, :],
                                      start=st_, stop=sp_), tS)

        def emit_bc(pend):
            nonlocal v_ah_last
            a_rr_p, qb_p, h_p = pend
            wait("tensor", aS, a_rr_p)
            t_bc = op("tensor", lambda e: e.matmul(PS4[:], ones1[:], LROW[:],
                                                   start=True, stop=True), tS)
            wait("vector", tS, t_bc)
            op("vector", lambda e: e.tensor_copy(RRB[:, 0, :], PS4[:]))
            qsl_p = slice(QS * qb_p, QS * qb_p + QS)
            v_ah_last = op("vector", lambda e, h_p=h_p, qsl_p=qsl_p:
                           e.tensor_mul(AH[:, h_p, qsl_p], PPV[:], RRB[:, 0, :]), vS)

        wait("tensor", vS, v_apply)
        for sbk in range(NSB):
            bank_free_v[sbk] = (vS, v_apply)

        for h in range(NH):
            tq = [None] * NSB
            proj(WQB, tq, ld_w[h][0])
            v_qf = evac_rope(QF, tq)
            if h + 1 < NH:
                wait("gpsimd", tS, tq[NSB - 1])
                ld_w[h + 1][0] = dma("gpsimd", iog, WQB[:], wq[h + 1])
            if h < NCH:
                emit_ffn_stat(h)
            tk = [None] * NSB
            proj(WKB, tk, ld_w[h][1])
            v_kf = evac_rope(KF, tk, prev_rope=v_qf)
            if h + 1 < NH:
                wait("gpsimd", tS, tk[NSB - 1])
                ld_w[h + 1][1] = dma("gpsimd", iog, WKB[:], wk[h + 1])
            wait("tensor", iog, ld_w[h][2])
            v_vh = None
            v_vt = None
            t_tr = None
            for half in range(2):
                banks = (2 * half, 2 * half + 1)
                for sbk in banks:
                    wait_bf("tensor", bank_free_v[sbk])
                    wait("tensor", aS, bank_exp_a[sbk])
                    bank_free_v[sbk] = None
                    bank_exp_a[sbk] = None
                t_vm = None
                for ic in range(NCH):
                    st_, sp_ = (ic == 0), (ic == NCH - 1)
                    if half == 1 and 2 <= ic < 10:
                        j = ic - 2
                        if ic == 2:
                            wait_bf("tensor", v_vt)
                        t_tr = op("tensor", lambda e, j=j:
                                  e.transpose(PS5B[:, 128 * j:128 * j + 128],
                                              VT[:, 128 * j:128 * j + 128], ident[:]),
                                  tS if j == 7 else None)
                    for sbk in banks:
                        sl = slice(512 * sbk, 512 * sbk + 512)
                        t_vm = op("tensor",
                                  lambda e, ic=ic, sl=sl, sbk=sbk, st_=st_, sp_=sp_:
                                  e.matmul(P[sbk][:], WVB[:, 128 * ic:128 * ic + 128],
                                           XN[:, ic, sl], start=st_, stop=sp_),
                                  tS if (sp_ and sbk == banks[1]) else None)
                if half == 0:
                    wait("scalar", vS, v_kf)
                    for j in range(2):
                        sbk = banks[j]
                        sl1 = slice(512 * j, 512 * j + 512)
                        wait("scalar", tS, t_vm)
                        v_vt = op("scalar", lambda e, sl1=sl1, sbk=sbk:
                                  e.activation(VT[:, sl1], P[sbk][:], AF.Copy), aS)
                        bank_free_v[sbk] = (aS, v_vt)
                        v_vt = (aS, v_vt)
                else:
                    wait("scalar", tS, t_tr)
                    v_vh = op("scalar", lambda e:
                              e.activation(VH[:, 0:8, :], PS5B[:], AF.Copy), aS)
                    v_vh = (aS, v_vh)
                    for j in range(2):
                        sbk = banks[j]
                        sl1 = slice(512 * j, 512 * j + 512)
                        wait("scalar", tS, t_vm)
                        v_vt = op("scalar", lambda e, sl1=sl1, sbk=sbk:
                                  e.activation(VT[:, sl1], P[sbk][:], AF.Copy), aS)
                        bank_free_v[sbk] = (aS, v_vt)
                        v_vt = (aS, v_vt)
            if h + 1 < NH:
                wait("gpsimd", tS, t_vm)
                ld_w[h + 1][2] = dma("gpsimd", iog, WVB[:], wv[h + 1])
            pend_tr = (v_vt, v_vh)

            vh_gate[0] = None
            wait("tensor", vS, v_qf)
            wait("tensor", vS, v_kf)
            for qb in range(NQB):
                qsl = slice(QS * qb, QS * qb + QS)
                ntc = 4 * qb + 4
                qb_g0 = g_ctr
                for tc in range(ntc):
                    g = g_ctr
                    g_ctr += 1
                    bank = g % 4
                    wait("tensor", aS, exp_marks.get(g - 4))
                    wait_bf("tensor", bank_free_v[bank])
                    bank_free_v[bank] = None
                    t_sc = op("tensor", lambda e, tc=tc, qsl=qsl, bank=bank:
                              e.matmul(P[bank][:], KF[:, 128 * tc:128 * tc + 128], QF[:, qsl],
                                       start=True, stop=True), tS)
                    slot = g % 4
                    j = tc - 4 * qb
                    wait("scalar", tS, t_sc)
                    wait("scalar", tS, pv_marks.get(g - 4))
                    is_diag = j >= 0
                    if j > 0:
                        op("scalar", lambda e, slot=slot, j=j: e.memzero(PT[:, slot, 0:128 * j]))
                    if is_diag:
                        px = g % 2
                        wait("scalar", vS, vm_marks.get(g - 2))
                        if j < 3:
                            op("scalar", lambda e, px=px, j=j, bank=bank:
                               e.activation(PTX[:, px, :], P[bank][:, 128 * j:128 * j + 128],
                                            AF.Exp, bias=NEGC[:], scale=SCALE))
                            a_e = op("scalar", lambda e, slot=slot, j=j, bank=bank:
                                     e.activation(PT[:, slot, 128 * (j + 1):QS],
                                                  P[bank][:, 128 * (j + 1):QS],
                                                  AF.Exp, bias=NEGC[:], scale=SCALE), aS)
                        else:
                            a_e = op("scalar", lambda e, px=px, j=j, bank=bank:
                                     e.activation(PTX[:, px, :], P[bank][:, 128 * j:128 * j + 128],
                                                  AF.Exp, bias=NEGC[:], scale=SCALE), aS)
                        wait("vector", aS, a_e)
                        vm_marks[g] = op("vector", lambda e, slot=slot, j=j, px=px:
                                         e.tensor_mul(PT[:, slot, 128 * j:128 * j + 128],
                                                      PTX[:, px, :], tri[:]), vS)
                    else:
                        a_e = op("scalar", lambda e, slot=slot, bank=bank:
                                 e.activation(PT[:, slot, :], P[bank][:], AF.Exp,
                                              bias=NEGC[:], scale=SCALE), aS)
                    exp_marks[g] = a_e
                    bank_exp_a[bank] = a_e
                    chunk_info[g] = (tc, slot, ntc, is_diag, tc == 0, tc == ntc - 1)
                    if tc == 0 and t_bc_pending is not None:
                        emit_bc(t_bc_pending)
                        t_bc_pending = None
                    if qb == 0 and tc in (1, 2, 3) and pend_tr is not None:
                        v_vt_p, v_vh_p = pend_tr
                        if tc == 1:
                            wait_bf("tensor", v_vt_p)
                            wait_bf("tensor", v_vh_p)
                        for jj in range(3 * (tc - 1), min(3 * tc, 8)):
                            t_tr2 = op("tensor", lambda e, jj=jj:
                                       e.transpose(PS5B[:, 128 * jj:128 * jj + 128],
                                                   VT[:, 128 * jj:128 * jj + 128], ident[:]),
                                       tS if jj == 7 else None)
                        if tc == 3:
                            pend_tr = None
                            wait("vector", tS, t_tr2)
                            v_vh2 = op("vector", lambda e:
                                       e.tensor_copy(VH[:, 8:16, :], PS5B[:]), vS)
                            vh_gate[0] = v_vh2
                    if tc >= 3:
                        emit_pv(g - 3)
                for gg in range(max(qb_g0, g_ctr - 3), g_ctr):
                    emit_pv(gg)
                wait("scalar", tS, pv_marks[g_ctr - 1])
                op("scalar", lambda e: e.activation(LROW[:], PL[0:1, :], AF.Ln))
                a_rr = op("scalar", lambda e: e.activation(LROW[:], LROW[:], AF.Exp, scale=-1.0), aS)
                t_bc_pending = (a_rr, qb, h)

            if DEBUG_OUT == "h0" and h == 0:
                wait("gpsimd", vS, max(v_qf, v_kf))
                wait("gpsimd", aS, aS.n)
                dma("gpsimd", iog, dbg[:, 0, :], QF[:])
                dma("gpsimd", iog, dbg[:, 1, :], KF[:])
                dma("gpsimd", iog, dbg[:, 2, :], VH[:])

        a_rr_p = t_bc_pending[0]
        emit_bc(t_bc_pending)
        t_bc_pending = None

        wait("tensor", gS, ffn_acc_v[0])
        wait("tensor", aS, a_rr_p)
        t_s = op("tensor", lambda e: e.matmul(PL[0:1, 0:1], ACC[:], onescolf[:], start=True, stop=True), tS)
        wait("scalar", tS, t_s)
        a_s = op("scalar", lambda e: e.activation(SVAL[:, 0:1], PL[0:1, 0:1], AF.Copy,
                                                  scale=1.0 / float(H * H)), aS)
        op("scalar", lambda e: e.activation(SVAL[:, 4:5], SVAL[:, 0:1], AF.Ln))
        a_i2s = op("scalar", lambda e: e.activation(SVAL[:, 2:3], SVAL[:, 4:5], AF.Exp,
                                                    bias=LN2V[:], scale=-1.0), aS)
        wait("vector", aS, a_i2s)
        op("vector", lambda e: e.tensor_scalar(SVAL[:, 1:2], SVAL[:, 0:1], 0.5, 0.5 * EPS_FFN, AL.mult, AL.add))
        v_sv = op("vector", lambda e: e.tensor_scalar(SVAL[:, 3:4], SVAL[:, 0:1], 0.5, None, AL.mult), vS)
        wait("tensor", vS, v_sv)
        op("tensor", lambda e: e.matmul(PL[:, 0:1], ones1[:], SVAL[0:1, 1:2], start=True, stop=True))
        op("tensor", lambda e: e.matmul(PL[:, 1:2], ones1[:], SVAL[0:1, 2:3], start=True, stop=True))
        t_b3 = op("tensor", lambda e: e.matmul(PL[:, 2:3], ones1[:], SVAL[0:1, 3:4], start=True, stop=True), tS)
        wait("vector", tS, t_b3)
        op("vector", lambda e: e.tensor_copy(TPOS[:], PL[:, 0:1]))
        op("vector", lambda e: e.tensor_scalar(TNEG[:], PL[:, 0:1], -1.0, None, AL.mult))
        op("vector", lambda e: e.tensor_copy(I2S[:], PL[:, 1:2]))
        v_cst = op("vector", lambda e: e.tensor_copy(S2K[:], PL[:, 2:3]), vS)

        ld_wo = [None] * NCH
        ld_wf = [None] * NCH
        ld_x = [None] * NCH
        wait("gpsimd", vS, v_ah_last)
        ld_wo[0] = dma("gpsimd", iog, WVB[:], wo[0])
        wait("sync", aS, ffn_abs[NCH - 1])
        ld_wf[0] = dma("sync", io, WS[:], wf[0])
        ld_wf[1] = dma("sync", io, WS2[:], wf[1])
        wait("scalar", vS, v_apply)
        ld_x[0] = dma("scalar", ios, RB[:], xt[0])

        woslab = [WQB, QR]
        w2slab = [WKB, QSW]
        sgslab = [QF, KF]
        t_pf = [t_s, t_s]
        v_w2 = [None] * NCH
        v_wo = [None] * NCH
        v_w2_prev = [None]
        ob_store = [None, None]
        v_ob = [None, None]

        def final_prep(ot):
            par = ot % 2
            wait("vector", iog, ld_wo[ot])
            wait("vector", tS, t_pf[par])
            v_wo[ot] = op("vector", lambda e, par=par:
                          e.tensor_scalar(woslab[par][:], WVB[:], I2S[:], None, AL.mult), vS)
            if ot + 1 < NCH:
                wait("gpsimd", vS, v_wo[ot])
                ld_wo[ot + 1] = dma("gpsimd", iog, WVB[:], wo[ot + 1])
            wait("scalar", io, ld_wf[ot])
            wait("scalar", vS, v_w2_prev[0] if v_w2_prev[0] is not None else v_cst)
            op("scalar", lambda e, ot=ot: e.activation(QF[:], xstage[ot % 2][:], AF.Sign, bias=TNEG[:]))
            a_sg = op("scalar", lambda e, ot=ot: e.activation(KF[:], xstage[ot % 2][:], AF.Sign, bias=TPOS[:]), aS)
            if ot + 2 < NCH:
                wait("sync", aS, a_sg)
                ld_wf[ot + 2] = dma("sync", io, xstage[ot % 2][:], wf[ot + 2])
            wait("vector", aS, a_sg)
            wait("vector", tS, t_pf[par])
            v_w2[ot] = op("vector", lambda e, par=par: e.tensor_add(w2slab[par][:], QF[:], KF[:]), vS)
            v_w2_prev[0] = v_w2[ot]

        final_prep(0)
        for ot in range(NCH):
            par = ot % 2
            if ot + 1 < NCH:
                final_prep(ot + 1)
            wait("tensor", vS, v_w2[ot])
            wait("tensor", vS, v_wo[ot])
            for pair in range(2):
                banks = (2 * pair, 2 * pair + 1)
                for sbk in banks:
                    wait_bf("tensor", bank_free_v[sbk])
                    wait("tensor", aS, bank_exp_a[sbk])
                    bank_free_v[sbk] = None
                    bank_exp_a[sbk] = None
                for ic in range(NCH):
                    for sbk in banks:
                        sl = slice(512 * sbk, 512 * sbk + 512)
                        op("tensor", lambda e, ic=ic, sl=sl, sbk=sbk, par=par:
                           e.matmul(P[sbk][:], woslab[par][:, 128 * ic:128 * ic + 128],
                                    AH[:, ic, sl], start=(ic == 0), stop=False))
                t_f = None
                for ic in range(NCH):
                    sp_ = (ic == NCH - 1)
                    for sbk in banks:
                        sl = slice(512 * sbk, 512 * sbk + 512)
                        t_f = op("tensor", lambda e, ic=ic, sl=sl, sbk=sbk, par=par, sp_=sp_:
                                 e.matmul(P[sbk][:], w2slab[par][:, 128 * ic:128 * ic + 128],
                                          XN[:, ic, sl], start=False, stop=sp_),
                                 tS if (sp_ and sbk == banks[1]) else None)
                t_pf[par] = t_f
                for sbk in banks:
                    sl = slice(512 * sbk, 512 * sbk + 512)
                    ob = sbk % 2
                    wait("vector", tS, t_f)
                    wait("vector", io, ob_store[ob])
                    wait("vector", ios, ld_x[ot])
                    wait("vector", io, b_bias)
                    op("vector", lambda e, sbk=sbk, ot=ot, ob=ob:
                       e.tensor_scalar(RRB[:, ob, :], P[sbk][:], S2K[:], bias_sb[:, ot:ot + 1],
                                       AL.mult, AL.add))
                    v_ob[ob] = op("vector", lambda e, sl=sl, ob=ob:
                                  e.tensor_add(RRB[:, ob, :], RRB[:, ob, :], RB[:, sl]), vS)
                    bank_free_v[sbk] = (vS, v_ob[ob])
                    wait("sync", vS, v_ob[ob])
                    ob_store[ob] = dma("sync", io, yt[ot][:, sl], RRB[:, ob, :])
            if ot + 1 < NCH:
                wait("scalar", vS, v_ob[1])
                ld_x[ot + 1] = dma("scalar", ios, RB[:], xt[ot + 1])

        if DEBUG_OUT == "xn":
            wait("gpsimd", vS, vS.n)
            for ic in range(NCH):
                dma("gpsimd", iog, dbg[:, ic, :], XN[:, ic, :])
        elif DEBUG_OUT == "ah":
            wait("gpsimd", vS, vS.n)
            for ic in range(NCH):
                dma("gpsimd", iog, dbg[:, ic, :], AH[:, ic, :])

        wait("sync", io, io.n)
        wait("gpsimd", iog, iog.n)
        wait("scalar", ios, ios.n)

        block = st.enter_context(nc.Block())

        def replay(which):
            def f(e):
                for eng, fn in events:
                    if eng == which:
                        fn(e)
            return f

        block.sync(replay("sync"))
        block.gpsimd(replay("gpsimd"))
        block.vector(replay("vector"))
        block.scalar(replay("scalar"))
        block.tensor(replay("tensor"))

    nc.finalize()
    return nc


_NC_CACHE = None


def kernel(L0_attn_x, L0_attn_Wq, L0_attn_Wk, L0_attn_Wv, L0_attn_Wo,
           L0_ffn_weight, L0_ffn_bias):
    global _NC_CACHE
    x = np.asarray(L0_attn_x)
    B = x.shape[0]

    def wtile(w):
        w = np.asarray(w)
        return np.ascontiguousarray(w.reshape(NH, 128, NCH, 128).transpose(0, 3, 2, 1))

    wqh = wtile(L0_attn_Wq)
    wkh = wtile(L0_attn_Wk)
    wvh = wtile(L0_attn_Wv)
    woh = wtile(L0_attn_Wo)
    wfh = wtile(L0_ffn_weight)
    bias_h = np.ascontiguousarray(np.asarray(L0_ffn_bias).reshape(NCH, 128).T)

    if _NC_CACHE is None:
        _NC_CACHE = build_program()
    nc = _NC_CACHE

    in_maps = []
    for b in range(B):
        xtb = np.ascontiguousarray(x[b].T).reshape(NCH, 128, S)
        in_maps.append({"xt": xtb, "wq": wqh, "wk": wkh, "wv": wvh,
                        "wo": woh, "wf": wfh, "bias_d": bias_h})

    res = run_bass_kernel_spmd(nc, in_maps, core_ids=list(range(B)))
    outs = []
    for b in range(B):
        ytb = np.asarray(res.results[b]["yt"])
        outs.append(ytb.transpose(2, 0, 1).reshape(S, H))
    return np.stack(outs, axis=0)


# version 11
# speedup vs baseline: 1.0598x; 1.0294x over previous
import math
from contextlib import ExitStack

import numpy as np
import ml_dtypes
import concourse.bass as bass
import concourse.mybir as mybir
from concourse.bass_utils import run_bass_kernel_spmd

F32 = mybir.dt.float32
BF16 = mybir.dt.bfloat16
I32 = mybir.dt.int32
AL = mybir.AluOpType
AF = mybir.ActivationFunctionType

S = 2048
H = 2048
NH = 16
HD = 128
NCH = 16
NQB = 4
QS = 512
NSB = 4
SCALE = 1.0 / math.sqrt(HD)
CSHIFT = -20.0
EPS_LN = 1e-5
EPS_FFN = 1e-10
PI = math.pi
TWO_PI = 2.0 * math.pi
_C1 = float(np.float32(6.28125))
_C2 = float(np.float32(np.float64(TWO_PI) - np.float64(_C1)))
_C3 = float(np.float32(np.float64(TWO_PI) - np.float64(_C1) - np.float64(_C2)))
LN1E4_64 = math.log(10000.0) / 64.0

DEBUG_OUT = None


class Sem:
    def __init__(self, handle):
        self.h = handle
        self.n = 0

    def inc(self, k=1):
        self.n += k
        return self.n


def build_program():
    nc = bass.Bass("TRN2", target_bir_lowering=False)

    xt = nc.dram_tensor("xt", [NCH, 128, S], F32, kind="ExternalInput")
    wq = nc.dram_tensor("wq", [NH, 128, NCH, 128], F32, kind="ExternalInput")
    wk = nc.dram_tensor("wk", [NH, 128, NCH, 128], F32, kind="ExternalInput")
    wv = nc.dram_tensor("wv", [NH, 128, NCH, 128], F32, kind="ExternalInput")
    wo = nc.dram_tensor("wo", [NCH, 128, NCH, 128], F32, kind="ExternalInput")
    wf = nc.dram_tensor("wf", [NCH, 128, NCH, 128], F32, kind="ExternalInput")
    bias_d = nc.dram_tensor("bias_d", [128, NCH], F32, kind="ExternalInput")
    yt = nc.dram_tensor("yt", [NCH, 128, S], F32, kind="ExternalOutput")
    dbg = None
    if DEBUG_OUT is not None:
        dbg = nc.dram_tensor("dbg", [128, NCH, S], F32, kind="ExternalOutput")

    events = []

    def ev(eng, fn):
        events.append((eng, fn))

    def wait(eng, sem, v):
        if v is None or v <= 0:
            return
        assert v <= sem.n, f"wait on future value {v} > {sem.n}"
        ev(eng, lambda e, h=sem.h, vv=v: e.wait_ge(h, vv))

    def dma(eng, sem, dst, src, slow=False):
        v = sem.inc(16)
        ev(eng, lambda e, d=dst, s_=src, h=sem.h, sl_=slow:
           e.dma_start(d, s_, allow_slow_non_contiguous=sl_).then_inc(h, 16))
        return v

    def op(eng, fn, sem=None):
        if sem is None:
            ev(eng, fn)
            return None
        v = sem.inc(1)
        ev(eng, lambda e, f=fn, h=sem.h: f(e).then_inc(h, 1))
        return v

    with ExitStack() as st:
        def sb(name, shape, dt):
            return st.enter_context(nc.sbuf_tensor(name, shape, dt))

        def ps(name, shape, dt=F32):
            return st.enter_context(nc.psum_tensor(name, shape, dt))

        XN = sb("XN", [128, NCH, S], BF16)
        AH = sb("AH", [128, NCH, S], BF16)
        WS = sb("WS", [128, S], F32)
        WS2 = sb("WS2", [128, S], F32)
        RB = sb("RB", [128, S], F32)
        WQB = sb("WQB", [128, S], BF16)
        WKB = sb("WKB", [128, S], BF16)
        WVB = sb("WVB", [128, S], BF16)
        QR = sb("QR", [128, S], BF16)
        QSW = sb("QSW", [128, S], BF16)
        QF = sb("QF", [128, S], BF16)
        KF = sb("KF", [128, S], BF16)
        VH = sb("VH", [128, NCH, 128], BF16)
        PT = sb("PT", [128, 4, QS], BF16)
        PTX = sb("PTX", [128, 2, 128], BF16)
        VT = sb("VT", [128, 1024], BF16)
        RRB = sb("RRB", [128, 2, QS], F32)
        KI = sb("KI", [128, 256], I32)
        LROW = sb("LROW", [1, QS], F32)
        TA = sb("TA", [128, S], BF16)
        TB = sb("TB", [128, S], BF16)
        onescol = sb("onescol", [128, 1], BF16)
        onescolf = sb("onescolf", [128, 1], F32)
        ones1 = sb("ones1", [1, 128], F32)
        tri = sb("tri", [128, 128], BF16)
        ident = sb("ident", [128, 128], BF16)
        bias_sb = sb("bias_sb", [128, NCH], F32)
        J64 = sb("J64", [128, 1], F32)
        INV = sb("INV", [128, 1], F32)
        SGN = sb("SGN", [128, 1], F32)
        NEGC = sb("NEGC", [128, 1], F32)
        EPSV = sb("EPSV", [1, 1], F32)
        LN2V = sb("LN2V", [1, 1], F32)
        ACC = sb("ACC", [128, 1], F32)
        ACC2 = sb("ACC2", [128, 1], F32)
        ACC3 = sb("ACC3", [128, 1], F32)
        SVAL = sb("SVAL", [1, 8], F32)
        TPOS = sb("TPOS", [128, 1], F32)
        TNEG = sb("TNEG", [128, 1], F32)
        S2K = sb("S2K", [128, 1], F32)
        I2S = sb("I2S", [128, 1], F32)

        P = [ps(f"P{i}", [128, 512]) for i in range(4)]
        PS4 = ps("PS4", [128, 512])
        PS5B = ps("PS5B", [128, 1024], BF16)
        PPV = ps("PPV", [128, 512])
        PL = ps("PL", [128, 512])

        io = Sem(st.enter_context(nc.semaphore("io")))
        iog = Sem(st.enter_context(nc.semaphore("iog")))
        ios = Sem(st.enter_context(nc.semaphore("ios")))
        tS = Sem(st.enter_context(nc.semaphore("tS")))
        vS = Sem(st.enter_context(nc.semaphore("vS")))
        aS = Sem(st.enter_context(nc.semaphore("aS")))
        gS = Sem(st.enter_context(nc.semaphore("gS")))

        xstage = [WS, WS2]

        def gp_setup(e):
            e.iota(WS2[:], [[1, S]], channel_multiplier=0, allow_small_or_imprecise_dtypes=True)
            e.iota(J64[0:64, :], [[0, 1]], channel_multiplier=1, allow_small_or_imprecise_dtypes=True)
            e.iota(J64[64:128, :], [[0, 1]], channel_multiplier=1, allow_small_or_imprecise_dtypes=True)
            e.memset(SGN[0:64, :], -1.0)
            e.memset(SGN[64:128, :], 1.0)
            e.memset(onescol[:], 1.0)
            e.memset(onescolf[:], 1.0)
            e.memset(ones1[:], 1.0)
            e.memset(NEGC[:], CSHIFT)
            e.memset(EPSV[:], EPS_LN)
            e.memset(LN2V[:], math.log(2.0))
            e.memset(QF[:, 0:128], 1.0)
            e.affine_select(tri[:], QF[:, 0:128], [[1, 128]], AL.is_ge, 0.0,
                            base=0, channel_multiplier=-1)
            return e.affine_select(ident[:], QF[:, 0:128], [[1, 128]], AL.is_equal, 0.0,
                                   base=0, channel_multiplier=-1)
        g_setup = op("gpsimd", gp_setup, gS)
        b_bias = dma("sync", io, bias_sb[:], bias_d[:])

        ld_w = [[None, None, None] for _ in range(NH)]
        wait("gpsimd", gS, g_setup)

        wait("scalar", gS, g_setup)
        a_inv = op("scalar", lambda e: e.activation(INV[:], J64[:], AF.Exp, scale=-LN1E4_64), aS)
        wait("vector", gS, g_setup)
        wait("vector", aS, a_inv)
        op("vector", lambda e: e.tensor_scalar(RB[:], WS2[:], INV[:], None, AL.mult))
        for cc in range(8):
            sl = slice(256 * cc, 256 * cc + 256)
            op("vector", lambda e, sl=sl: e.tensor_scalar(WS2[:, sl], RB[:, sl], 1.0 / TWO_PI, None, AL.mult))
            op("vector", lambda e, sl=sl: e.tensor_copy(KI[:], WS2[:, sl]))
            op("vector", lambda e, sl=sl: e.tensor_copy(WS2[:, sl], KI[:]))
        op("vector", lambda e: e.scalar_tensor_tensor(WS[:], WS2[:], _C1, RB[:], AL.mult, AL.subtract))
        op("vector", lambda e: e.scalar_tensor_tensor(RB[:], WS2[:], _C2, WS[:], AL.mult, AL.add))
        op("vector", lambda e: e.scalar_tensor_tensor(WS[:], WS2[:], _C3, RB[:], AL.mult, AL.add))
        op("vector", lambda e: e.tensor_scalar(WS2[:], WS[:], -PI, TWO_PI, AL.is_lt, AL.mult))
        op("vector", lambda e: e.tensor_add(RB[:], WS[:], WS2[:]))
        op("vector", lambda e: e.tensor_scalar(WS2[:], RB[:], PI, TWO_PI, AL.is_gt, AL.mult))
        v_rs = op("vector", lambda e: e.tensor_sub(WS[:], RB[:], WS2[:]), vS)
        wait("scalar", vS, v_rs)
        a_sin = op("scalar", lambda e: e.activation(TA[:], WS[:], AF.Sin, scale=-1.0), aS)
        wait("vector", aS, a_sin)
        op("vector", lambda e: e.tensor_scalar(RB[:], WS[:], PI / 2.0, None, AL.add))
        op("vector", lambda e: e.tensor_scalar(WS2[:], RB[:], PI, TWO_PI, AL.is_gt, AL.mult))
        v_rc = op("vector", lambda e: e.tensor_sub(WS[:], RB[:], WS2[:]), vS)
        wait("scalar", vS, v_rc)
        a_cos = op("scalar", lambda e: e.activation(QR[:], WS[:], AF.Sin), aS)
        wait("vector", aS, a_cos)
        v_tab = op("vector", lambda e: e.tensor_scalar(TB[:], QR[:], SGN[:], None, AL.mult), vS)

        d_x = [None] * NCH
        for ic in range(NCH):
            d_x[ic] = dma("gpsimd", iog, AH[:, ic, :], xt[ic])
        ld_w[0][0] = dma("gpsimd", iog, WQB[:], wq[0])
        ld_w[0][1] = dma("gpsimd", iog, WKB[:], wk[0])
        ld_w[0][2] = dma("gpsimd", iog, WVB[:], wv[0])
        v_ep = None
        for sbk in range(NSB):
            sl = slice(512 * sbk, 512 * sbk + 512)
            sq_marks = [None, None]
            for ic in range(NCH):
                wait("scalar", tS, sq_marks[ic % 2])
                wait("scalar", iog, d_x[ic])
                if ic == 0 and v_ep is not None:
                    wait("scalar", vS, v_ep)
                a_sq = op("scalar", lambda e, ic=ic, sl=sl:
                          e.activation(PT[:, ic % 2, :], AH[:, ic, sl], AF.Square), aS)
                wait("tensor", aS, a_sq)
                st_, sp_ = (ic == 0), (ic == NCH - 1)
                wait("tensor", iog, d_x[ic])
                op("tensor", lambda e, ic=ic, sl=sl, st_=st_, sp_=sp_:
                   e.matmul(PS4[0:1, :], onescol[:], AH[:, ic, sl], start=st_, stop=sp_))
                sq_marks[ic % 2] = op("tensor", lambda e, ic=ic, st_=st_, sp_=sp_:
                                      e.matmul(PL[0:1, :], onescol[:], PT[:, ic % 2, :],
                                               start=st_, stop=sp_), tS)
            t_mm = sq_marks[(NCH - 1) % 2]
            wait("vector", tS, t_mm)
            op("vector", lambda e: e.tensor_scalar(LROW[:], PS4[0:1, :], 1.0 / H, None, AL.mult))
            op("vector", lambda e: e.tensor_scalar(RRB[0:1, 0, :], PL[0:1, :], 1.0 / H, None, AL.mult))
            op("vector", lambda e: e.tensor_mul(PS4[0:1, :], LROW[:], LROW[:]))
            v_var = op("vector", lambda e: e.tensor_sub(RRB[0:1, 0, :], RRB[0:1, 0, :], PS4[0:1, :]), vS)
            wait("scalar", vS, v_var)
            op("scalar", lambda e: e.activation(RRB[0:1, 0, :], RRB[0:1, 0, :], AF.Ln, bias=EPSV[:]))
            a_r = op("scalar", lambda e: e.activation(RRB[0:1, 0, :], RRB[0:1, 0, :], AF.Exp, scale=-0.5), aS)
            wait("tensor", aS, a_r)
            op("tensor", lambda e, sbk=sbk: e.matmul(P[sbk][:], ones1[:], LROW[:], start=True, stop=True))
            t_r = op("tensor", lambda e: e.matmul(PS4[:], ones1[:], RRB[0:1, 0, :], start=True, stop=True), tS)
            wait("vector", tS, t_r)
            v_ep = op("vector", lambda e, sl=sl: e.tensor_copy(RB[:, sl], PS4[:]), vS)
            wait("tensor", vS, v_ep)

        d_x = [None] * NCH
        wait("sync", vS, max(v_ep, v_tab))
        d_x[0] = dma("sync", io, WS[:], xt[0])
        d_x[1] = dma("sync", io, WS2[:], xt[1])
        v_xn = [None] * NCH
        for ic in range(NCH):
            wait("vector", io, d_x[ic])
            for sbk in range(NSB):
                sl = slice(512 * sbk, 512 * sbk + 512)
                op("vector", lambda e, ic=ic, sl=sl, sbk=sbk:
                   e.tensor_sub(RRB[:, 1, :], xstage[ic % 2][:, sl], P[sbk][:]))
                v_xn[ic] = op("vector", lambda e, ic=ic, sl=sl:
                              e.tensor_mul(XN[:, ic, sl], RRB[:, 1, :], RB[:, sl]), vS)
            if ic + 2 < NCH:
                wait("sync", vS, v_xn[ic])
                d_x[ic + 2] = dma("sync", io, xstage[ic % 2][:], xt[ic + 2])
        v_apply = v_xn[NCH - 1]

        ffn_d = [None] * NCH
        ffn_abs = [None] * NCH
        ffn_acc_v = [None]

        def emit_ffn_stat(ot):
            if ot < 2:
                wait("sync", vS, v_apply)
            else:
                wait("sync", aS, ffn_abs[ot - 2])
            ffn_d[ot] = dma("sync", io, xstage[ot % 2][:], wf[ot])
            wait("scalar", io, ffn_d[ot])
            for pc in range(4):
                psl = slice(512 * pc, 512 * pc + 512)
                first = (ot == 0 and pc == 0)
                acc_t = ACC if first else (ACC2 if pc % 2 == 0 else ACC3)
                if not first:
                    wait("scalar", gS, ffn_acc_v[0])
                ffn_abs[ot] = op("scalar", lambda e, ot=ot, psl=psl, acc_t=acc_t:
                                 e.activation(RRB[:, 1, :], xstage[ot % 2][:, psl], AF.Abs,
                                              accum_out=acc_t[:]), aS)
                if not first:
                    wait("gpsimd", aS, ffn_abs[ot])
                    ffn_acc_v[0] = op("gpsimd", lambda e, acc_t=acc_t:
                                      e.tensor_add(ACC[:], ACC[:], acc_t[:]), gS)

        bank_free_v = [None] * 4
        bank_exp_a = [None] * 4

        def wait_bf(eng, bf):
            if bf is not None:
                wait(eng, bf[0], bf[1])
        exp_marks = {}
        pv_marks = {}
        vm_marks = {}
        chunk_info = {}
        g_ctr = 0
        v_ah_last = None
        t_bc_pending = None

        def proj(wbuf, marks, ldmark):
            wait("tensor", iog, ldmark)
            for pair in range(2):
                banks = (2 * pair, 2 * pair + 1)
                for sbk in banks:
                    wait_bf("tensor", bank_free_v[sbk])
                    wait("tensor", aS, bank_exp_a[sbk])
                    bank_free_v[sbk] = None
                    bank_exp_a[sbk] = None
                for ic in range(NCH):
                    st_, sp_ = (ic == 0), (ic == NCH - 1)
                    for sbk in banks:
                        sl = slice(512 * sbk, 512 * sbk + 512)
                        m = op("tensor",
                               lambda e, wbuf=wbuf, ic=ic, sl=sl, sbk=sbk, st_=st_, sp_=sp_:
                               e.matmul(P[sbk][:], wbuf[:, 128 * ic:128 * ic + 128],
                                        XN[:, ic, sl], start=st_, stop=sp_),
                               tS if (sp_ and sbk == banks[1]) else None)
                        if sp_:
                            marks[sbk] = None
                marks[2 * pair] = m
                marks[2 * pair + 1] = m

        def evac_rope(dstF, marks, prev_rope=None):
            wait("scalar", vS, prev_rope)
            av = None
            for sbk in range(NSB):
                sl = slice(512 * sbk, 512 * sbk + 512)
                wait("scalar", tS, marks[sbk])
                av = op("scalar", lambda e, sl=sl, sbk=sbk: e.activation(QR[:, sl], P[sbk][:], AF.Copy), aS)
                bank_free_v[sbk] = (aS, av)
            wait("gpsimd", aS, av)
            dma("gpsimd", iog, QSW[0:64, :], QR[64:128, :])
            dsw = dma("gpsimd", iog, QSW[64:128, :], QR[0:64, :])
            wait("vector", aS, av)
            op("vector", lambda e, dstF=dstF: e.tensor_mul(dstF[:], QR[:], TA[:]))
            wait("vector", iog, dsw)
            op("vector", lambda e: e.tensor_mul(QR[:], QSW[:], TB[:]))
            return op("vector", lambda e, dstF=dstF: e.tensor_add(dstF[:], dstF[:], QR[:]), vS)

        vh_gate = [None]

        def emit_pv(g):
            tc, slot, ntc, is_diag, st_, sp_ = chunk_info[g]
            if is_diag:
                wait("tensor", vS, vm_marks[g])
            else:
                wait("tensor", aS, exp_marks[g])
            if tc == 8:
                wait("tensor", vS, vh_gate[0])
            if st_:
                wait("tensor", vS, v_ah_last)
            op("tensor", lambda e, tc=tc, slot=slot, st_=st_, sp_=sp_:
               e.matmul(PPV[:], VH[:, tc, :], PT[:, slot, :], start=st_, stop=sp_))
            pv_marks[g] = op("tensor", lambda e, slot=slot, st_=st_, sp_=sp_:
                             e.matmul(PL[0:1, :], onescol[:], PT[:, slot, :],
                                      start=st_, stop=sp_), tS)

        def emit_bc(pend):
            nonlocal v_ah_last
            a_rr_p, qb_p, h_p = pend
            wait("tensor", aS, a_rr_p)
            t_bc = op("tensor", lambda e: e.matmul(PS4[:], ones1[:], LROW[:],
                                                   start=True, stop=True), tS)
            wait("vector", tS, t_bc)
            op("vector", lambda e: e.tensor_copy(RRB[:, 0, :], PS4[:]))
            qsl_p = slice(QS * qb_p, QS * qb_p + QS)
            v_ah_last = op("vector", lambda e, h_p=h_p, qsl_p=qsl_p:
                           e.tensor_mul(AH[:, h_p, qsl_p], PPV[:], RRB[:, 0, :]), vS)

        wait("tensor", vS, v_apply)
        for sbk in range(NSB):
            bank_free_v[sbk] = (vS, v_apply)

        for h in range(NH):
            tq = [None] * NSB
            proj(WQB, tq, ld_w[h][0])
            v_qf = evac_rope(QF, tq)
            if h + 1 < NH:
                wait("gpsimd", tS, tq[NSB - 1])
                ld_w[h + 1][0] = dma("gpsimd", iog, WQB[:], wq[h + 1])
            if h < NCH:
                emit_ffn_stat(h)
            tk = [None] * NSB
            proj(WKB, tk, ld_w[h][1])
            v_kf = evac_rope(KF, tk, prev_rope=v_qf)
            if h + 1 < NH:
                wait("gpsimd", tS, tk[NSB - 1])
                ld_w[h + 1][1] = dma("gpsimd", iog, WKB[:], wk[h + 1])
            wait("tensor", iog, ld_w[h][2])
            v_vh = None
            v_vt = None
            t_tr = None
            for half in range(2):
                banks = (2 * half, 2 * half + 1)
                for sbk in banks:
                    wait_bf("tensor", bank_free_v[sbk])
                    wait("tensor", aS, bank_exp_a[sbk])
                    bank_free_v[sbk] = None
                    bank_exp_a[sbk] = None
                t_vm = None
                for ic in range(NCH):
                    st_, sp_ = (ic == 0), (ic == NCH - 1)
                    if half == 1 and 2 <= ic < 10:
                        j = ic - 2
                        if ic == 2:
                            wait_bf("tensor", v_vt)
                        t_tr = op("tensor", lambda e, j=j:
                                  e.transpose(PS5B[:, 128 * j:128 * j + 128],
                                              VT[:, 128 * j:128 * j + 128], ident[:]),
                                  tS if j == 7 else None)
                    for sbk in banks:
                        sl = slice(512 * sbk, 512 * sbk + 512)
                        t_vm = op("tensor",
                                  lambda e, ic=ic, sl=sl, sbk=sbk, st_=st_, sp_=sp_:
                                  e.matmul(P[sbk][:], WVB[:, 128 * ic:128 * ic + 128],
                                           XN[:, ic, sl], start=st_, stop=sp_),
                                  tS if (sp_ and sbk == banks[1]) else None)
                if half == 0:
                    for j in range(2):
                        sbk = banks[j]
                        sl1 = slice(512 * j, 512 * j + 512)
                        wait("scalar", tS, t_vm)
                        v_vt = op("scalar", lambda e, sl1=sl1, sbk=sbk:
                                  e.activation(VT[:, sl1], P[sbk][:], AF.Copy), aS)
                        bank_free_v[sbk] = (aS, v_vt)
                        v_vt = (aS, v_vt)
                else:
                    wait("scalar", tS, t_tr)
                    v_vh = op("scalar", lambda e:
                              e.activation(VH[:, 0:8, :], PS5B[:], AF.Copy), aS)
                    v_vh = (aS, v_vh)
                    for j in range(2):
                        sbk = banks[j]
                        sl1 = slice(512 * j, 512 * j + 512)
                        wait("scalar", tS, t_vm)
                        v_vt = op("scalar", lambda e, sl1=sl1, sbk=sbk:
                                  e.activation(VT[:, sl1], P[sbk][:], AF.Copy), aS)
                        bank_free_v[sbk] = (aS, v_vt)
                        v_vt = (aS, v_vt)
            if h + 1 < NH:
                wait("gpsimd", tS, t_vm)
                ld_w[h + 1][2] = dma("gpsimd", iog, WVB[:], wv[h + 1])
            pend_tr = (v_vt, v_vh)

            vh_gate[0] = None
            wait("tensor", vS, v_qf)
            wait("tensor", vS, v_kf)
            for qb in range(NQB):
                qsl = slice(QS * qb, QS * qb + QS)
                ntc = 4 * qb + 4
                qb_g0 = g_ctr
                for tc in range(ntc):
                    g = g_ctr
                    g_ctr += 1
                    bank = g % 4
                    wait("tensor", aS, exp_marks.get(g - 4))
                    wait_bf("tensor", bank_free_v[bank])
                    bank_free_v[bank] = None
                    t_sc = op("tensor", lambda e, tc=tc, qsl=qsl, bank=bank:
                              e.matmul(P[bank][:], KF[:, 128 * tc:128 * tc + 128], QF[:, qsl],
                                       start=True, stop=True), tS)
                    slot = g % 4
                    j = tc - 4 * qb
                    wait("scalar", tS, t_sc)
                    wait("scalar", tS, pv_marks.get(g - 4))
                    is_diag = j >= 0
                    if j > 0:
                        op("scalar", lambda e, slot=slot, j=j: e.memzero(PT[:, slot, 0:128 * j]))
                    if is_diag:
                        px = g % 2
                        wait("scalar", vS, vm_marks.get(g - 2))
                        if j < 3:
                            op("scalar", lambda e, px=px, j=j, bank=bank:
                               e.activation(PTX[:, px, :], P[bank][:, 128 * j:128 * j + 128],
                                            AF.Exp, bias=NEGC[:], scale=SCALE))
                            a_e = op("scalar", lambda e, slot=slot, j=j, bank=bank:
                                     e.activation(PT[:, slot, 128 * (j + 1):QS],
                                                  P[bank][:, 128 * (j + 1):QS],
                                                  AF.Exp, bias=NEGC[:], scale=SCALE), aS)
                        else:
                            a_e = op("scalar", lambda e, px=px, j=j, bank=bank:
                                     e.activation(PTX[:, px, :], P[bank][:, 128 * j:128 * j + 128],
                                                  AF.Exp, bias=NEGC[:], scale=SCALE), aS)
                        wait("vector", aS, a_e)
                        vm_marks[g] = op("vector", lambda e, slot=slot, j=j, px=px:
                                         e.tensor_mul(PT[:, slot, 128 * j:128 * j + 128],
                                                      PTX[:, px, :], tri[:]), vS)
                    else:
                        a_e = op("scalar", lambda e, slot=slot, bank=bank:
                                 e.activation(PT[:, slot, :], P[bank][:], AF.Exp,
                                              bias=NEGC[:], scale=SCALE), aS)
                    exp_marks[g] = a_e
                    bank_exp_a[bank] = a_e
                    chunk_info[g] = (tc, slot, ntc, is_diag, tc == 0, tc == ntc - 1)
                    if tc == min(1, ntc - 1) and t_bc_pending is not None:
                        emit_bc(t_bc_pending)
                        t_bc_pending = None
                    if qb == 0 and tc in (1, 2, 3) and pend_tr is not None:
                        v_vt_p, v_vh_p = pend_tr
                        if tc == 1:
                            wait_bf("tensor", v_vt_p)
                            wait_bf("tensor", v_vh_p)
                        for jj in range(3 * (tc - 1), min(3 * tc, 8)):
                            t_tr2 = op("tensor", lambda e, jj=jj:
                                       e.transpose(PS5B[:, 128 * jj:128 * jj + 128],
                                                   VT[:, 128 * jj:128 * jj + 128], ident[:]),
                                       tS if jj == 7 else None)
                        if tc == 3:
                            pend_tr = None
                            wait("vector", tS, t_tr2)
                            v_vh2 = op("vector", lambda e:
                                       e.tensor_copy(VH[:, 8:16, :], PS5B[:]), vS)
                            vh_gate[0] = v_vh2
                    if tc >= 3:
                        emit_pv(g - 3)
                for gg in range(max(qb_g0, g_ctr - 3), g_ctr):
                    emit_pv(gg)
                wait("scalar", tS, pv_marks[g_ctr - 1])
                op("scalar", lambda e: e.activation(LROW[:], PL[0:1, :], AF.Ln))
                a_rr = op("scalar", lambda e: e.activation(LROW[:], LROW[:], AF.Exp, scale=-1.0), aS)
                t_bc_pending = (a_rr, qb, h)

            if DEBUG_OUT == "h0" and h == 0:
                wait("gpsimd", vS, max(v_qf, v_kf))
                wait("gpsimd", aS, aS.n)
                dma("gpsimd", iog, dbg[:, 0, :], QF[:])
                dma("gpsimd", iog, dbg[:, 1, :], KF[:])
                dma("gpsimd", iog, dbg[:, 2, :], VH[:])

        a_rr_p = t_bc_pending[0]
        emit_bc(t_bc_pending)
        t_bc_pending = None

        wait("tensor", gS, ffn_acc_v[0])
        wait("tensor", aS, a_rr_p)
        t_s = op("tensor", lambda e: e.matmul(PL[0:1, 0:1], ACC[:], onescolf[:], start=True, stop=True), tS)
        wait("scalar", tS, t_s)
        a_s = op("scalar", lambda e: e.activation(SVAL[:, 0:1], PL[0:1, 0:1], AF.Copy,
                                                  scale=1.0 / float(H * H)), aS)
        op("scalar", lambda e: e.activation(SVAL[:, 4:5], SVAL[:, 0:1], AF.Ln))
        a_i2s = op("scalar", lambda e: e.activation(SVAL[:, 2:3], SVAL[:, 4:5], AF.Exp,
                                                    bias=LN2V[:], scale=-1.0), aS)
        wait("vector", aS, a_i2s)
        op("vector", lambda e: e.tensor_scalar(SVAL[:, 1:2], SVAL[:, 0:1], 0.5, 0.5 * EPS_FFN, AL.mult, AL.add))
        v_sv = op("vector", lambda e: e.tensor_scalar(SVAL[:, 3:4], SVAL[:, 0:1], 0.5, None, AL.mult), vS)
        wait("tensor", vS, v_sv)
        op("tensor", lambda e: e.matmul(PL[:, 0:1], ones1[:], SVAL[0:1, 1:2], start=True, stop=True))
        op("tensor", lambda e: e.matmul(PL[:, 1:2], ones1[:], SVAL[0:1, 2:3], start=True, stop=True))
        t_b3 = op("tensor", lambda e: e.matmul(PL[:, 2:3], ones1[:], SVAL[0:1, 3:4], start=True, stop=True), tS)
        wait("vector", tS, t_b3)
        op("vector", lambda e: e.tensor_copy(TPOS[:], PL[:, 0:1]))
        op("vector", lambda e: e.tensor_scalar(TNEG[:], PL[:, 0:1], -1.0, None, AL.mult))
        op("vector", lambda e: e.tensor_copy(I2S[:], PL[:, 1:2]))
        v_cst = op("vector", lambda e: e.tensor_copy(S2K[:], PL[:, 2:3]), vS)

        ld_wo = [None] * NCH
        ld_wf = [None] * NCH
        ld_x = [None] * NCH
        wait("gpsimd", vS, v_ah_last)
        ld_wo[0] = dma("gpsimd", iog, WVB[:], wo[0])
        wait("sync", aS, ffn_abs[NCH - 1])
        ld_wf[0] = dma("sync", io, WS[:], wf[0])
        ld_wf[1] = dma("sync", io, WS2[:], wf[1])
        wait("scalar", vS, v_apply)
        ld_x[0] = dma("scalar", ios, RB[:], xt[0])

        woslab = [WQB, QR]
        w2slab = [WKB, QSW]
        sgslab = [QF, KF]
        t_pf = [t_s, t_s]
        v_w2 = [None] * NCH
        v_wo = [None] * NCH
        v_w2_prev = [None]
        ob_store = [None, None]
        v_ob = [None, None]

        def final_prep(ot):
            par = ot % 2
            wait("vector", iog, ld_wo[ot])
            wait("vector", tS, t_pf[par])
            v_wo[ot] = op("vector", lambda e, par=par:
                          e.tensor_scalar(woslab[par][:], WVB[:], I2S[:], None, AL.mult), vS)
            if ot + 1 < NCH:
                wait("gpsimd", vS, v_wo[ot])
                ld_wo[ot + 1] = dma("gpsimd", iog, WVB[:], wo[ot + 1])
            wait("scalar", io, ld_wf[ot])
            wait("scalar", vS, v_w2_prev[0] if v_w2_prev[0] is not None else v_cst)
            op("scalar", lambda e, ot=ot: e.activation(QF[:], xstage[ot % 2][:], AF.Sign, bias=TNEG[:]))
            a_sg = op("scalar", lambda e, ot=ot: e.activation(KF[:], xstage[ot % 2][:], AF.Sign, bias=TPOS[:]), aS)
            if ot + 2 < NCH:
                wait("sync", aS, a_sg)
                ld_wf[ot + 2] = dma("sync", io, xstage[ot % 2][:], wf[ot + 2])
            wait("vector", aS, a_sg)
            wait("vector", tS, t_pf[par])
            v_w2[ot] = op("vector", lambda e, par=par: e.tensor_add(w2slab[par][:], QF[:], KF[:]), vS)
            v_w2_prev[0] = v_w2[ot]

        final_prep(0)
        for ot in range(NCH):
            par = ot % 2
            if ot + 1 < NCH:
                final_prep(ot + 1)
            wait("tensor", vS, v_w2[ot])
            wait("tensor", vS, v_wo[ot])
            for pair in range(2):
                banks = (2 * pair, 2 * pair + 1)
                for sbk in banks:
                    wait_bf("tensor", bank_free_v[sbk])
                    wait("tensor", aS, bank_exp_a[sbk])
                    bank_free_v[sbk] = None
                    bank_exp_a[sbk] = None
                for ic in range(NCH):
                    for sbk in banks:
                        sl = slice(512 * sbk, 512 * sbk + 512)
                        op("tensor", lambda e, ic=ic, sl=sl, sbk=sbk, par=par:
                           e.matmul(P[sbk][:], woslab[par][:, 128 * ic:128 * ic + 128],
                                    AH[:, ic, sl], start=(ic == 0), stop=False))
                t_f = None
                for ic in range(NCH):
                    sp_ = (ic == NCH - 1)
                    for sbk in banks:
                        sl = slice(512 * sbk, 512 * sbk + 512)
                        t_f = op("tensor", lambda e, ic=ic, sl=sl, sbk=sbk, par=par, sp_=sp_:
                                 e.matmul(P[sbk][:], w2slab[par][:, 128 * ic:128 * ic + 128],
                                          XN[:, ic, sl], start=False, stop=sp_),
                                 tS if (sp_ and sbk == banks[1]) else None)
                t_pf[par] = t_f
                for sbk in banks:
                    sl = slice(512 * sbk, 512 * sbk + 512)
                    ob = sbk % 2
                    wait("vector", tS, t_f)
                    wait("vector", io, ob_store[ob])
                    wait("vector", ios, ld_x[ot])
                    wait("vector", io, b_bias)
                    op("vector", lambda e, sbk=sbk, ot=ot, ob=ob:
                       e.tensor_scalar(RRB[:, ob, :], P[sbk][:], S2K[:], bias_sb[:, ot:ot + 1],
                                       AL.mult, AL.add))
                    v_ob[ob] = op("vector", lambda e, sl=sl, ob=ob:
                                  e.tensor_add(RRB[:, ob, :], RRB[:, ob, :], RB[:, sl]), vS)
                    bank_free_v[sbk] = (vS, v_ob[ob])
                    wait("sync", vS, v_ob[ob])
                    ob_store[ob] = dma("sync", io, yt[ot][:, sl], RRB[:, ob, :])
            if ot + 1 < NCH:
                wait("scalar", vS, v_ob[1])
                ld_x[ot + 1] = dma("scalar", ios, RB[:], xt[ot + 1])

        if DEBUG_OUT == "xn":
            wait("gpsimd", vS, vS.n)
            for ic in range(NCH):
                dma("gpsimd", iog, dbg[:, ic, :], XN[:, ic, :])
        elif DEBUG_OUT == "ah":
            wait("gpsimd", vS, vS.n)
            for ic in range(NCH):
                dma("gpsimd", iog, dbg[:, ic, :], AH[:, ic, :])

        wait("sync", io, io.n)
        wait("gpsimd", iog, iog.n)
        wait("scalar", ios, ios.n)

        block = st.enter_context(nc.Block())

        def replay(which):
            def f(e):
                for eng, fn in events:
                    if eng == which:
                        fn(e)
            return f

        block.sync(replay("sync"))
        block.gpsimd(replay("gpsimd"))
        block.vector(replay("vector"))
        block.scalar(replay("scalar"))
        block.tensor(replay("tensor"))

    nc.finalize()
    return nc


_NC_CACHE = None


def kernel(L0_attn_x, L0_attn_Wq, L0_attn_Wk, L0_attn_Wv, L0_attn_Wo,
           L0_ffn_weight, L0_ffn_bias):
    global _NC_CACHE
    x = np.asarray(L0_attn_x)
    B = x.shape[0]

    def wtile(w):
        w = np.asarray(w)
        return np.ascontiguousarray(w.reshape(NH, 128, NCH, 128).transpose(0, 3, 2, 1))

    wqh = wtile(L0_attn_Wq)
    wkh = wtile(L0_attn_Wk)
    wvh = wtile(L0_attn_Wv)
    woh = wtile(L0_attn_Wo)
    wfh = wtile(L0_ffn_weight)
    bias_h = np.ascontiguousarray(np.asarray(L0_ffn_bias).reshape(NCH, 128).T)

    if _NC_CACHE is None:
        _NC_CACHE = build_program()
    nc = _NC_CACHE

    in_maps = []
    for b in range(B):
        xtb = np.ascontiguousarray(x[b].T).reshape(NCH, 128, S)
        in_maps.append({"xt": xtb, "wq": wqh, "wk": wkh, "wv": wvh,
                        "wo": woh, "wf": wfh, "bias_d": bias_h})

    res = run_bass_kernel_spmd(nc, in_maps, core_ids=list(range(B)))
    outs = []
    for b in range(B):
        ytb = np.asarray(res.results[b]["yt"])
        outs.append(ytb.transpose(2, 0, 1).reshape(S, H))
    return np.stack(outs, axis=0)


# version 18
# speedup vs baseline: 1.0925x; 1.0308x over previous
import math
from contextlib import ExitStack

import numpy as np
import ml_dtypes
import concourse.bass as bass
import concourse.mybir as mybir
from concourse.bass_utils import run_bass_kernel_spmd

F32 = mybir.dt.float32
BF16 = mybir.dt.bfloat16
I32 = mybir.dt.int32
AL = mybir.AluOpType
AF = mybir.ActivationFunctionType

S = 2048
H = 2048
NH = 16
HD = 128
NCH = 16
NQB = 4
QS = 512
NSB = 4
SCALE = 1.0 / math.sqrt(HD)
CSHIFT = -20.0
EPS_LN = 1e-5
EPS_FFN = 1e-10
PI = math.pi
TWO_PI = 2.0 * math.pi
_C1 = float(np.float32(6.28125))
_C2 = float(np.float32(np.float64(TWO_PI) - np.float64(_C1)))
_C3 = float(np.float32(np.float64(TWO_PI) - np.float64(_C1) - np.float64(_C2)))
LN1E4_64 = math.log(10000.0) / 64.0

DEBUG_OUT = None


class Sem:
    def __init__(self, handle):
        self.h = handle
        self.n = 0

    def inc(self, k=1):
        self.n += k
        return self.n


def build_program():
    nc = bass.Bass("TRN2", target_bir_lowering=False)

    xt = nc.dram_tensor("xt", [NCH, 128, S], F32, kind="ExternalInput")
    wq = nc.dram_tensor("wq", [NH, 128, NCH, 128], F32, kind="ExternalInput")
    wk = nc.dram_tensor("wk", [NH, 128, NCH, 128], F32, kind="ExternalInput")
    wv = nc.dram_tensor("wv", [NH, 128, NCH, 128], F32, kind="ExternalInput")
    wo = nc.dram_tensor("wo", [NCH, 128, NCH, 128], F32, kind="ExternalInput")
    wf = nc.dram_tensor("wf", [NCH, 128, NCH, 128], F32, kind="ExternalInput")
    bias_d = nc.dram_tensor("bias_d", [128, NCH], F32, kind="ExternalInput")
    yt = nc.dram_tensor("yt", [NCH, 128, S], F32, kind="ExternalOutput")
    dbg = None
    if DEBUG_OUT is not None:
        dbg = nc.dram_tensor("dbg", [128, NCH, S], F32, kind="ExternalOutput")

    events = []

    def ev(eng, fn):
        events.append((eng, fn))

    def wait(eng, sem, v):
        if v is None or v <= 0:
            return
        assert v <= sem.n, f"wait on future value {v} > {sem.n}"
        ev(eng, lambda e, h=sem.h, vv=v: e.wait_ge(h, vv))

    def dma(eng, sem, dst, src, slow=False):
        v = sem.inc(16)
        ev(eng, lambda e, d=dst, s_=src, h=sem.h, sl_=slow:
           e.dma_start(d, s_, allow_slow_non_contiguous=sl_).then_inc(h, 16))
        return v

    def op(eng, fn, sem=None):
        if sem is None:
            ev(eng, fn)
            return None
        v = sem.inc(1)
        ev(eng, lambda e, f=fn, h=sem.h: f(e).then_inc(h, 1))
        return v

    with ExitStack() as st:
        def sb(name, shape, dt):
            return st.enter_context(nc.sbuf_tensor(name, shape, dt))

        def ps(name, shape, dt=F32):
            return st.enter_context(nc.psum_tensor(name, shape, dt))

        XN = sb("XN", [128, NCH, S], BF16)
        AH = sb("AH", [128, NCH, S], BF16)
        WS = sb("WS", [128, S], F32)
        WS2 = sb("WS2", [128, S], F32)
        RB = sb("RB", [128, S], F32)
        WQB = sb("WQB", [128, S], BF16)
        WKB = sb("WKB", [128, S], BF16)
        WVB = sb("WVB", [128, S], BF16)
        QR = sb("QR", [128, S], BF16)
        QSW = sb("QSW", [128, S], BF16)
        QF = sb("QF", [128, S], BF16)
        KF = sb("KF", [128, S], BF16)
        VH = sb("VH", [128, NCH, 128], BF16)
        PT = sb("PT", [128, 4, QS], BF16)
        PTX = sb("PTX", [128, 2, 128], BF16)
        PTP = sb("PTP", [128, 2, QS], BF16)
        VT = sb("VT", [128, 1024], BF16)
        RRB = sb("RRB", [128, 2, QS], F32)
        KI = sb("KI", [128, 256], I32)
        TA = sb("TA", [128, S], BF16)
        TB = sb("TB", [128, S], BF16)
        onescol = sb("onescol", [128, 1], BF16)
        onescolf = sb("onescolf", [128, 1], F32)
        ones1 = sb("ones1", [1, 128], F32)
        ones1b = sb("ones1b", [1, 128], BF16)
        tri = sb("tri", [128, 128], BF16)
        ident = sb("ident", [128, 128], BF16)
        bias_sb = sb("bias_sb", [128, NCH], F32)
        J64 = sb("J64", [128, 1], F32)
        INV = sb("INV", [128, 1], F32)
        SGN = sb("SGN", [128, 1], F32)
        NEGC = sb("NEGC", [128, 1], F32)
        ACC = sb("ACC", [128, 1], F32)
        ACC2 = sb("ACC2", [128, 1], F32)
        SVAL = sb("SVAL", [1, 8], F32)
        TPOS = sb("TPOS", [128, 1], F32)
        TNEG = sb("TNEG", [128, 1], F32)
        S2K = sb("S2K", [128, 1], F32)
        I2S = sb("I2S", [128, 1], F32)

        PSC = ps("PSC", [128, 2048])

        class BankView:
            def __init__(self, base, off):
                self.base = base
                self.off = off

            def __getitem__(self, idx):
                if idx == slice(None):
                    return self.base[:, self.off:self.off + 512]
                p, c = idx
                assert p == slice(None)
                return self.base[:, self.off + (c.start or 0):self.off + c.stop]

        P = [BankView(PSC, 512 * i) for i in range(4)]
        PS4 = ps("PS4", [128, 512])
        PS5B = ps("PS5B", [128, 1024], BF16)
        PPV = ps("PPV", [128, 512])
        PL = ps("PL", [128, 512])

        io = Sem(st.enter_context(nc.semaphore("io")))
        iog = Sem(st.enter_context(nc.semaphore("iog")))
        ios = Sem(st.enter_context(nc.semaphore("ios")))
        tS = Sem(st.enter_context(nc.semaphore("tS")))
        vS = Sem(st.enter_context(nc.semaphore("vS")))
        aS = Sem(st.enter_context(nc.semaphore("aS")))
        gS = Sem(st.enter_context(nc.semaphore("gS")))

        xstage = [WS, WS2]

        def gp_setup(e):
            e.iota(WS2[:], [[1, S]], channel_multiplier=0, allow_small_or_imprecise_dtypes=True)
            e.iota(J64[0:64, :], [[0, 1]], channel_multiplier=1, allow_small_or_imprecise_dtypes=True)
            e.iota(J64[64:128, :], [[0, 1]], channel_multiplier=1, allow_small_or_imprecise_dtypes=True)
            e.memset(SGN[0:64, :], -1.0)
            e.memset(SGN[64:128, :], 1.0)
            e.memset(onescol[:], 1.0)
            e.memset(onescolf[:], 1.0)
            e.memset(ones1[:], 1.0)
            e.memset(ones1b[:], 1.0)
            e.memset(NEGC[:], CSHIFT)
            e.memset(SVAL[0:1, 5:6], EPS_LN)
            e.memset(SVAL[0:1, 6:7], math.log(2.0))
            e.memset(QF[:, 0:128], 1.0)
            e.affine_select(tri[:], QF[:, 0:128], [[1, 128]], AL.is_ge, 0.0,
                            base=0, channel_multiplier=-1)
            return e.affine_select(ident[:], QF[:, 0:128], [[1, 128]], AL.is_equal, 0.0,
                                   base=0, channel_multiplier=-1)
        g_setup = op("gpsimd", gp_setup, gS)
        b_bias = dma("sync", io, bias_sb[:], bias_d[:])

        ld_w = [[None, None, None] for _ in range(NH)]
        wait("gpsimd", gS, g_setup)

        wait("scalar", gS, g_setup)
        a_inv = op("scalar", lambda e: e.activation(INV[:], J64[:], AF.Exp, scale=-LN1E4_64), aS)
        wait("vector", gS, g_setup)
        wait("vector", aS, a_inv)
        op("vector", lambda e: e.tensor_scalar(RB[:], WS2[:], INV[:], None, AL.mult))
        for cc in range(8):
            sl = slice(256 * cc, 256 * cc + 256)
            op("vector", lambda e, sl=sl: e.tensor_scalar(WS2[:, sl], RB[:, sl], 1.0 / TWO_PI, None, AL.mult))
            op("vector", lambda e, sl=sl: e.tensor_copy(KI[:], WS2[:, sl]))
            op("vector", lambda e, sl=sl: e.tensor_copy(WS2[:, sl], KI[:]))
        op("vector", lambda e: e.scalar_tensor_tensor(WS[:], WS2[:], _C1, RB[:], AL.mult, AL.subtract))
        op("vector", lambda e: e.scalar_tensor_tensor(RB[:], WS2[:], _C2, WS[:], AL.mult, AL.add))
        op("vector", lambda e: e.scalar_tensor_tensor(WS[:], WS2[:], _C3, RB[:], AL.mult, AL.add))
        op("vector", lambda e: e.tensor_scalar(WS2[:], WS[:], -PI, TWO_PI, AL.is_lt, AL.mult))
        op("vector", lambda e: e.tensor_add(RB[:], WS[:], WS2[:]))
        op("vector", lambda e: e.tensor_scalar(WS2[:], RB[:], PI, TWO_PI, AL.is_gt, AL.mult))
        v_rs = op("vector", lambda e: e.tensor_sub(WS[:], RB[:], WS2[:]), vS)
        wait("scalar", vS, v_rs)
        a_sin = op("scalar", lambda e: e.activation(TA[:], WS[:], AF.Sin, scale=-1.0), aS)
        wait("vector", aS, a_sin)
        op("vector", lambda e: e.tensor_scalar(RB[:], WS[:], PI / 2.0, None, AL.add))
        op("vector", lambda e: e.tensor_scalar(WS2[:], RB[:], PI, TWO_PI, AL.is_gt, AL.mult))
        v_rc = op("vector", lambda e: e.tensor_sub(WS[:], RB[:], WS2[:]), vS)
        wait("scalar", vS, v_rc)
        a_cos = op("scalar", lambda e: e.activation(QR[:], WS[:], AF.Sin), aS)
        wait("vector", aS, a_cos)
        v_tab = op("vector", lambda e: e.tensor_scalar(TB[:], QR[:], SGN[:], None, AL.mult), vS)

        d_x = [None] * NCH
        for ic in range(NCH):
            d_x[ic] = dma("gpsimd", iog, AH[:, ic, :], xt[ic])
        ld_w[0][0] = dma("gpsimd", iog, WQB[:], wq[0])
        ld_w[0][1] = dma("gpsimd", iog, WKB[:], wk[0])
        ld_w[0][2] = dma("gpsimd", iog, WVB[:], wv[0])
        xn_marks = [[None] * NCH for _ in range(NSB)]
        apply_sub_last = [None]
        dq_x = [None] * 64
        wait("sync", vS, v_tab)
        for q in range(8):
            slot = q % 8
            dst = [WS, WS2][slot // 4][:, 512 * (slot % 4):512 * (slot % 4) + 512]
            dq_x[q] = dma("sync", io, dst, xt[q][:, 0:512])
        v_ep = None
        ep_mark = None
        mbc_marks = [None] * NSB
        rbev_marks = [None] * NSB
        for sbk in range(NSB):
            sl = slice(512 * sbk, 512 * sbk + 512)
            sq_marks = [None, None]
            for ic in range(NCH):
                wait("scalar", tS, sq_marks[ic % 2])
                wait("scalar", iog, d_x[ic])
                if ic == 0 and v_ep is not None:
                    wait("scalar", vS, v_ep)
                a_sq = op("scalar", lambda e, ic=ic, sl=sl:
                          e.activation(PT[:, ic % 2, :], AH[:, ic, sl], AF.Square), aS)
                wait("tensor", aS, a_sq)
                st_, sp_ = (ic == 0), (ic == NCH - 1)
                wait("tensor", iog, d_x[ic])
                if ic == 0 and ep_mark is not None:
                    wait("tensor", vS, ep_mark)
                op("tensor", lambda e, ic=ic, sl=sl, st_=st_, sp_=sp_:
                   e.matmul(PSC[0:1, 0:512], onescol[:], AH[:, ic, sl], start=st_, stop=sp_))
                sq_marks[ic % 2] = op("tensor", lambda e, ic=ic, st_=st_, sp_=sp_:
                                      e.matmul(PSC[0:1, 512:1024], onescol[:], PT[:, ic % 2, :],
                                               start=st_, stop=sp_), tS)
            t_mm = sq_marks[(NCH - 1) % 2]
            wait("vector", tS, t_mm)
            op("vector", lambda e: e.tensor_scalar(RRB[0:1, 1, :], PSC[0:1, 0:512], 1.0 / H, None, AL.mult))
            op("vector", lambda e: e.tensor_scalar(RRB[0:1, 0, :], PSC[0:1, 512:1024], 1.0 / H, None, AL.mult))
            op("vector", lambda e: e.tensor_mul(PSC[0:1, 0:512], RRB[0:1, 1, :], RRB[0:1, 1, :]))
            v_var = op("vector", lambda e: e.tensor_sub(RRB[0:1, 0, :], RRB[0:1, 0, :], PSC[0:1, 0:512]), vS)
            ep_mark = v_var
            wait("scalar", vS, v_var)
            op("scalar", lambda e: e.activation(RRB[0:1, 0, :], RRB[0:1, 0, :], AF.Ln, bias=SVAL[0:1, 5:6]))
            a_r = op("scalar", lambda e: e.activation(RRB[0:1, 0, :], RRB[0:1, 0, :], AF.Exp, scale=-0.5), aS)
            wait("tensor", aS, a_r)
            if sbk > 0:
                wait("tensor", vS, apply_sub_last[0])
            mbc_marks[sbk] = op("tensor", lambda e: e.matmul(PS4[:], ones1[:], RRB[0:1, 1, :],
                                                             start=True, stop=True), tS)
            t_r = op("tensor", lambda e: e.matmul(PL[:], ones1[:], RRB[0:1, 0, :],
                                                  start=True, stop=True), tS)
            wait("vector", tS, t_r)
            v_ep = op("vector", lambda e, sl=sl: e.tensor_copy(RB[:, sl], PL[:]), vS)
            rbev_marks[sbk] = v_ep
            wait("tensor", vS, v_ep)

            for ic in range(NCH):
                q = 16 * sbk + ic
                slot = q % 8
                qsl = [WS, WS2][slot // 4][:, 512 * (slot % 4):512 * (slot % 4) + 512]
                wait("vector", io, dq_x[q])
                wait("vector", tS, mbc_marks[sbk])
                op("vector", lambda e, qsl=qsl: e.tensor_sub(RRB[:, 1, :], qsl, PS4[:]))
                xn_marks[sbk][ic] = op("vector", lambda e, ic=ic, sl=sl:
                                       e.tensor_mul(XN[:, ic, sl], RRB[:, 1, :], RB[:, sl]), vS)
                apply_sub_last[0] = xn_marks[sbk][ic]
                if q + 8 < 64:
                    wait("sync", vS, xn_marks[sbk][ic])
                    q2 = q + 8
                    slot2 = q2 % 8
                    dst2 = [WS, WS2][slot2 // 4][:, 512 * (slot2 % 4):512 * (slot2 % 4) + 512]
                    dq_x[q2] = dma("sync", io, dst2,
                                   xt[q2 % 16][:, slice(512 * (q2 // 16), 512 * (q2 // 16) + 512)])
        v_apply = xn_marks[NSB - 1][NCH - 1]

        ffn_d = [None] * NCH
        ffn_abs = [None] * NCH
        ffn_acc_v = [None]

        def emit_ffn_stat(ot):
            if ot < 2:
                wait("sync", vS, v_apply)
            else:
                wait("sync", aS, ffn_abs[ot - 2])
            ffn_d[ot] = dma("sync", io, xstage[ot % 2][:], wf[ot])
            wait("scalar", io, ffn_d[ot])
            for pc in range(4):
                psl = slice(512 * pc, 512 * pc + 512)
                first = (ot == 0 and pc == 0)
                acc_t = ACC if first else (ACC2 if pc % 2 == 0 else TPOS)
                if not first:
                    wait("scalar", gS, ffn_acc_v[0])
                ffn_abs[ot] = op("scalar", lambda e, ot=ot, psl=psl, acc_t=acc_t:
                                 e.activation(RRB[:, 1, :], xstage[ot % 2][:, psl], AF.Abs,
                                              accum_out=acc_t[:]), aS)
                if not first:
                    wait("gpsimd", aS, ffn_abs[ot])
                    ffn_acc_v[0] = op("gpsimd", lambda e, acc_t=acc_t:
                                      e.tensor_add(ACC[:], ACC[:], acc_t[:]), gS)

        bank_free_v = [None] * 4
        bank_exp_a = [None] * 4

        def wait_bf(eng, bf):
            if bf is not None:
                wait(eng, bf[0], bf[1])
        exp_marks = {}
        pv_marks = {}
        vm_marks = {}
        vpair_marks = {}
        lpair_marks = {}
        chunk_info = {}
        g_ctr = 0
        v_ah_last = None
        t_bc_pending = None

        def proj(wbuf, marks, ldmark, mid=None, gates=None):
            wait("tensor", iog, ldmark)
            for pair in range(2):
                if pair == 1 and mid is not None:
                    mid()
                banks = (2 * pair, 2 * pair + 1)
                for sbk in banks:
                    wait_bf("tensor", bank_free_v[sbk])
                    wait("tensor", aS, bank_exp_a[sbk])
                    bank_free_v[sbk] = None
                    bank_exp_a[sbk] = None
                for ic in range(NCH):
                    st_, sp_ = (ic == 0), (ic == NCH - 1)
                    for sbk in banks:
                        sl = slice(512 * sbk, 512 * sbk + 512)
                        if gates is not None:
                            wait("tensor", vS, gates[sbk][ic])
                        m = op("tensor",
                               lambda e, wbuf=wbuf, ic=ic, sl=sl, sbk=sbk, st_=st_, sp_=sp_:
                               e.matmul(P[sbk][:], wbuf[:, 128 * ic:128 * ic + 128],
                                        XN[:, ic, sl], start=st_, stop=sp_),
                               tS if (sp_ and sbk == banks[1]) else None)
                        if sp_:
                            marks[sbk] = None
                marks[2 * pair] = m
                marks[2 * pair + 1] = m

        def evac_rope(dstF, marks, prev_rope=None):
            wait("scalar", vS, prev_rope)
            av = None
            for sbk in range(NSB):
                sl = slice(512 * sbk, 512 * sbk + 512)
                wait("scalar", tS, marks[sbk])
                av = op("scalar", lambda e, sl=sl, sbk=sbk: e.activation(QR[:, sl], P[sbk][:], AF.Copy), aS)
                bank_free_v[sbk] = (aS, av)
            wait("gpsimd", aS, av)
            dma("gpsimd", iog, QSW[0:64, :], QR[64:128, :])
            dsw = dma("gpsimd", iog, QSW[64:128, :], QR[0:64, :])
            wait("vector", aS, av)
            op("vector", lambda e, dstF=dstF: e.tensor_mul(dstF[:], QR[:], TA[:]))
            wait("vector", iog, dsw)
            op("vector", lambda e: e.tensor_mul(QR[:], QSW[:], TB[:]))
            return op("vector", lambda e, dstF=dstF: e.tensor_add(dstF[:], dstF[:], QR[:]), vS)

        vh_gate = [None]
        rr_pending = [None]
        rr_guard = [None]
        P3V = P[3]
        PVACC = [PPV, P3V]
        qb_parity = [0]
        ppv_free = [None, None]

        def emit_pv(g, ppv_cur):
            tc, slot, ntc, is_diag, st_, sp_ = chunk_info[g]
            if is_diag:
                wait("tensor", vS, vm_marks[g])
            else:
                wait("tensor", aS, exp_marks[g])
            if tc == 8:
                wait("tensor", vS, vh_gate[0])
            if st_:
                wait_bf("tensor", ppv_free[ppv_cur is P3V])
                if ppv_cur is P3V:
                    wait_bf("tensor", bank_free_v[3])
                    bank_free_v[3] = None
            pv_marks[g] = op("tensor", lambda e, tc=tc, slot=slot, st_=st_, sp_=sp_, pc=ppv_cur:
                             e.matmul(pc[:], VH[:, tc, :], PT[:, slot, :], start=st_, stop=sp_),
                             tS)
            if tc % 2 == 1:
                p = g // 2
                pi = p % 2
                wait("tensor", vS, vpair_marks[g])
                if tc == 1:
                    wait("tensor", aS, rr_guard[0])
                lpair_marks[p] = op("tensor", lambda e, pi=pi, st2=(tc == 1), sp_=sp_:
                                    e.matmul(PL[0:1, :], onescol[:], PTP[:, pi, :],
                                             start=st2, stop=sp_), tS)

        def emit_bc(pend):
            nonlocal v_ah_last
            a_rr_p, qb_p, h_p, pc = pend
            wait("tensor", aS, a_rr_p)
            t_bc = op("tensor", lambda e: e.matmul(PS4[:], ones1[:], RRB[0:1, 0, :],
                                                   start=True, stop=True), tS)
            wait("vector", tS, t_bc)
            op("vector", lambda e: e.tensor_copy(RRB[:, 0, :], PS4[:]))
            qsl_p = slice(QS * qb_p, QS * qb_p + QS)
            v_ah_last = op("vector", lambda e, h_p=h_p, qsl_p=qsl_p, pc=pc:
                           e.tensor_mul(AH[:, h_p, qsl_p], pc[:], RRB[:, 0, :]), vS)
            ppv_free[pc is P3V] = (vS, v_ah_last)
            if pc is P3V:
                bank_free_v[3] = (vS, v_ah_last)

        bank_free_v[0] = (vS, ep_mark)
        bank_free_v[1] = (vS, ep_mark)

        for h in range(NH):
            pend_rr = rr_pending[0]
            rr_pending[0] = None
            if pend_rr is not None:
                lp, qb_p, h_p, pc_p = pend_rr
                wait("scalar", tS, lp)
                op("scalar", lambda e: e.activation(RRB[0:1, 0, :], PL[0:1, :], AF.Ln))
                a_rr = op("scalar", lambda e: e.activation(RRB[0:1, 0, :], RRB[0:1, 0, :],
                                                           AF.Exp, scale=-1.0), aS)
                rr_guard[0] = a_rr
                t_bc_pending = (a_rr, qb_p, h_p, pc_p)
            pend = t_bc_pending
            t_bc_pending = None

            def bc_mid(pend=pend):
                if pend is not None:
                    emit_bc(pend)
            tq = [None] * NSB
            proj(WQB, tq, ld_w[h][0], mid=bc_mid,
                 gates=(xn_marks if h == 0 else None))
            v_qf = evac_rope(QF, tq)
            if h + 1 < NH:
                wait("gpsimd", tS, tq[NSB - 1])
                ld_w[h + 1][0] = dma("gpsimd", iog, WQB[:], wq[h + 1])
            if h < NCH:
                emit_ffn_stat(h)
            tk = [None] * NSB
            proj(WKB, tk, ld_w[h][1])
            v_kf = evac_rope(KF, tk, prev_rope=v_qf)
            if h + 1 < NH:
                wait("gpsimd", tS, tk[NSB - 1])
                ld_w[h + 1][1] = dma("gpsimd", iog, WKB[:], wk[h + 1])
            wait("tensor", iog, ld_w[h][2])
            v_vh = None
            v_vt = None
            t_tr = None
            for half in range(2):
                banks = (2 * half, 2 * half + 1)
                for sbk in banks:
                    wait_bf("tensor", bank_free_v[sbk])
                    wait("tensor", aS, bank_exp_a[sbk])
                    bank_free_v[sbk] = None
                    bank_exp_a[sbk] = None
                t_vm = None
                for ic in range(NCH):
                    st_, sp_ = (ic == 0), (ic == NCH - 1)
                    if half == 1 and 2 <= ic < 10:
                        j = ic - 2
                        if ic == 2:
                            wait_bf("tensor", v_vt)
                        t_tr = op("tensor", lambda e, j=j:
                                  e.transpose(PS5B[:, 128 * j:128 * j + 128],
                                              VT[:, 128 * j:128 * j + 128], ident[:]),
                                  tS if j == 7 else None)
                    for sbk in banks:
                        sl = slice(512 * sbk, 512 * sbk + 512)
                        t_vm = op("tensor",
                                  lambda e, ic=ic, sl=sl, sbk=sbk, st_=st_, sp_=sp_:
                                  e.matmul(P[sbk][:], WVB[:, 128 * ic:128 * ic + 128],
                                           XN[:, ic, sl], start=st_, stop=sp_),
                                  tS if (sp_ and sbk == banks[1]) else None)
                if half == 0:
                    for j in range(2):
                        sbk = banks[j]
                        sl1 = slice(512 * j, 512 * j + 512)
                        wait("scalar", tS, t_vm)
                        v_vt = op("scalar", lambda e, sl1=sl1, sbk=sbk:
                                  e.activation(VT[:, sl1], P[sbk][:], AF.Copy), aS)
                        bank_free_v[sbk] = (aS, v_vt)
                        v_vt = (aS, v_vt)
                else:
                    wait("scalar", tS, t_tr)
                    v_vh = op("scalar", lambda e:
                              e.activation(VH[:, 0:8, :], PS5B[:], AF.Copy), aS)
                    v_vh = (aS, v_vh)
                    for j in range(2):
                        sbk = banks[j]
                        sl1 = slice(512 * j, 512 * j + 512)
                        wait("scalar", tS, t_vm)
                        v_vt = op("scalar", lambda e, sl1=sl1, sbk=sbk:
                                  e.activation(VT[:, sl1], P[sbk][:], AF.Copy), aS)
                        bank_free_v[sbk] = (aS, v_vt)
                        v_vt = (aS, v_vt)
            if h + 1 < NH:
                wait("gpsimd", tS, t_vm)
                ld_w[h + 1][2] = dma("gpsimd", iog, WVB[:], wv[h + 1])
            pend_tr = (v_vt, v_vh)

            vh_gate[0] = None
            wait("tensor", vS, v_qf)
            wait("tensor", vS, v_kf)
            for qb in range(NQB):
                qsl = slice(QS * qb, QS * qb + QS)
                ntc = 4 * qb + 4
                qb_g0 = g_ctr
                ppv_cur = PVACC[qb_parity[0]]
                qb_parity[0] ^= 1
                for tc in range(ntc):
                    g = g_ctr
                    g_ctr += 1
                    bank = g % 3
                    wait("tensor", aS, exp_marks.get(g - 3))
                    wait_bf("tensor", bank_free_v[bank])
                    bank_free_v[bank] = None
                    t_sc = op("tensor", lambda e, tc=tc, qsl=qsl, bank=bank:
                              e.matmul(P[bank][:], KF[:, 128 * tc:128 * tc + 128], QF[:, qsl],
                                       start=True, stop=True), tS)
                    slot = g % 4
                    j = tc - 4 * qb
                    is_diag = j >= 0
                    wait("scalar", tS, t_sc)
                    wait("scalar", tS, pv_marks.get(g - 4))
                    gp4 = g - 4
                    if gp4 >= 0 and (gp4 | 1) in vpair_marks:
                        wait("scalar", vS, vpair_marks[gp4 | 1])
                    if j > 0:
                        op("scalar", lambda e, slot=slot, j=j: e.memzero(PT[:, slot, 0:128 * j]))
                    if is_diag:
                        px = g % 2
                        wait("scalar", vS, vm_marks.get(g - 2))
                        if j < 3:
                            op("scalar", lambda e, px=px, j=j, bank=bank:
                               e.activation(PTX[:, px, :], P[bank][:, 128 * j:128 * j + 128],
                                            AF.Exp, bias=NEGC[:], scale=SCALE))
                            a_e = op("scalar", lambda e, slot=slot, j=j, bank=bank:
                                     e.activation(PT[:, slot, 128 * (j + 1):QS],
                                                  P[bank][:, 128 * (j + 1):QS],
                                                  AF.Exp, bias=NEGC[:], scale=SCALE), aS)
                        else:
                            a_e = op("scalar", lambda e, px=px, j=j, bank=bank:
                                     e.activation(PTX[:, px, :], P[bank][:, 128 * j:128 * j + 128],
                                                  AF.Exp, bias=NEGC[:], scale=SCALE), aS)
                        wait("vector", aS, a_e)
                        vm_marks[g] = op("vector", lambda e, slot=slot, j=j, px=px:
                                         e.tensor_mul(PT[:, slot, 128 * j:128 * j + 128],
                                                      PTX[:, px, :], tri[:]), vS)
                    else:
                        a_e = op("scalar", lambda e, slot=slot, bank=bank:
                                 e.activation(PT[:, slot, :], P[bank][:], AF.Exp,
                                              bias=NEGC[:], scale=SCALE), aS)
                    exp_marks[g] = a_e
                    bank_exp_a[bank] = a_e
                    chunk_info[g] = (tc, slot, ntc, is_diag, tc == 0, tc == ntc - 1)
                    if tc % 2 == 1:
                        p = g // 2
                        pi = p % 2
                        for gg in (g - 1, g):
                            if chunk_info[gg][3]:
                                wait("vector", vS, vm_marks[gg])
                            else:
                                wait("vector", aS, exp_marks[gg])
                        wait("vector", tS, lpair_marks.get(p - 2))
                        s0 = chunk_info[g - 1][1]
                        vpair_marks[g] = op("vector", lambda e, pi=pi, s0=s0, s1=slot:
                                            e.tensor_add(PTP[:, pi, :], PT[:, s0, :],
                                                         PT[:, s1, :]), vS)
                    if tc == min(3, ntc - 2) and rr_pending[0] is not None:
                        lp, qb_p, h_p, pc_p = rr_pending[0]
                        rr_pending[0] = None
                        wait("scalar", tS, lp)
                        op("scalar", lambda e: e.activation(RRB[0:1, 0, :], PL[0:1, :], AF.Ln))
                        a_rr = op("scalar", lambda e: e.activation(RRB[0:1, 0, :], RRB[0:1, 0, :],
                                                                   AF.Exp, scale=-1.0), aS)
                        rr_guard[0] = a_rr
                        t_bc_pending = (a_rr, qb_p, h_p, pc_p)
                    if tc == min(5, ntc - 1) and t_bc_pending is not None:
                        emit_bc(t_bc_pending)
                        t_bc_pending = None
                    if qb == 0 and tc in (1, 2, 3) and pend_tr is not None:
                        v_vt_p, v_vh_p = pend_tr
                        if tc == 1:
                            wait_bf("tensor", v_vt_p)
                            wait_bf("tensor", v_vh_p)
                        for jj in range(3 * (tc - 1), min(3 * tc, 8)):
                            t_tr2 = op("tensor", lambda e, jj=jj:
                                       e.transpose(PS5B[:, 128 * jj:128 * jj + 128],
                                                   VT[:, 128 * jj:128 * jj + 128], ident[:]),
                                       tS if jj == 7 else None)
                        if tc == 3:
                            pend_tr = None
                            wait("vector", tS, t_tr2)
                            v_vh2 = op("vector", lambda e:
                                       e.tensor_copy(VH[:, 8:16, :], PS5B[:]), vS)
                            vh_gate[0] = v_vh2
                    if tc >= 3:
                        emit_pv(g - 3, ppv_cur)
                for gg in range(max(qb_g0, g_ctr - 3), g_ctr):
                    emit_pv(gg, ppv_cur)
                rr_pending[0] = (lpair_marks[(g_ctr - 1) // 2], qb, h, ppv_cur)

            if DEBUG_OUT == "h0" and h == 0:
                wait("gpsimd", vS, max(v_qf, v_kf))
                wait("gpsimd", aS, aS.n)
                dma("gpsimd", iog, dbg[:, 0, :], QF[:])
                dma("gpsimd", iog, dbg[:, 1, :], KF[:])
                dma("gpsimd", iog, dbg[:, 2, :], VH[:])

        lp, qb_p, h_p, pc_p = rr_pending[0]
        rr_pending[0] = None
        wait("scalar", tS, lp)
        op("scalar", lambda e: e.activation(RRB[0:1, 0, :], PL[0:1, :], AF.Ln))
        a_rr_p = op("scalar", lambda e: e.activation(RRB[0:1, 0, :], RRB[0:1, 0, :],
                                                     AF.Exp, scale=-1.0), aS)
        emit_bc((a_rr_p, qb_p, h_p, pc_p))
        t_bc_pending = None

        wait("tensor", gS, ffn_acc_v[0])
        wait("tensor", aS, a_rr_p)
        t_s = op("tensor", lambda e: e.matmul(PL[0:1, 0:1], ACC[:], onescolf[:], start=True, stop=True), tS)
        wait("scalar", tS, t_s)
        a_s = op("scalar", lambda e: e.activation(SVAL[:, 0:1], PL[0:1, 0:1], AF.Copy,
                                                  scale=1.0 / float(H * H)), aS)
        op("scalar", lambda e: e.activation(SVAL[:, 4:5], SVAL[:, 0:1], AF.Ln))
        a_i2s = op("scalar", lambda e: e.activation(SVAL[:, 2:3], SVAL[:, 4:5], AF.Exp,
                                                    bias=SVAL[0:1, 6:7], scale=-1.0), aS)
        wait("vector", aS, a_i2s)
        op("vector", lambda e: e.tensor_scalar(SVAL[:, 1:2], SVAL[:, 0:1], 0.5, 0.5 * EPS_FFN, AL.mult, AL.add))
        v_sv = op("vector", lambda e: e.tensor_scalar(SVAL[:, 3:4], SVAL[:, 0:1], 0.5, None, AL.mult), vS)
        wait("tensor", vS, v_sv)
        op("tensor", lambda e: e.matmul(PL[:, 0:1], ones1[:], SVAL[0:1, 1:2], start=True, stop=True))
        op("tensor", lambda e: e.matmul(PL[:, 1:2], ones1[:], SVAL[0:1, 2:3], start=True, stop=True))
        t_b3 = op("tensor", lambda e: e.matmul(PL[:, 2:3], ones1[:], SVAL[0:1, 3:4], start=True, stop=True), tS)
        wait("vector", tS, t_b3)
        op("vector", lambda e: e.tensor_copy(TPOS[:], PL[:, 0:1]))
        op("vector", lambda e: e.tensor_scalar(TNEG[:], PL[:, 0:1], -1.0, None, AL.mult))
        op("vector", lambda e: e.tensor_copy(I2S[:], PL[:, 1:2]))
        v_cst = op("vector", lambda e: e.tensor_copy(S2K[:], PL[:, 2:3]), vS)

        ld_wo = [None] * NCH
        ld_wf = [None] * NCH
        ld_x = [None] * NCH
        wait("gpsimd", vS, v_ah_last)
        ld_wo[0] = dma("gpsimd", iog, WVB[:], wo[0])
        wait("sync", aS, ffn_abs[NCH - 1])
        ld_wf[0] = dma("sync", io, WS[:], wf[0])
        ld_wf[1] = dma("sync", io, WS2[:], wf[1])
        wait("scalar", vS, v_apply)
        ld_x[0] = dma("scalar", ios, RB[:], xt[0])

        woslab = [WQB, QR]
        w2slab = [WKB, QSW]
        sgslab = [QF, KF]
        t_pf = [t_s, t_s]
        v_w2 = [None] * NCH
        v_wo = [None] * NCH
        v_w2_prev = [None]
        ob_store = [None, None]
        v_ob = [None, None]

        def final_prep(ot):
            par = ot % 2
            wait("vector", iog, ld_wo[ot])
            wait("vector", tS, t_pf[par])
            v_wo[ot] = op("vector", lambda e, par=par:
                          e.tensor_scalar(woslab[par][:], WVB[:], I2S[:], None, AL.mult), vS)
            if ot + 1 < NCH:
                wait("gpsimd", vS, v_wo[ot])
                ld_wo[ot + 1] = dma("gpsimd", iog, WVB[:], wo[ot + 1])
            wait("scalar", io, ld_wf[ot])
            wait("scalar", vS, v_w2_prev[0] if v_w2_prev[0] is not None else v_cst)
            op("scalar", lambda e, ot=ot: e.activation(QF[:], xstage[ot % 2][:], AF.Sign, bias=TNEG[:]))
            a_sg = op("scalar", lambda e, ot=ot: e.activation(KF[:], xstage[ot % 2][:], AF.Sign, bias=TPOS[:]), aS)
            if ot + 2 < NCH:
                wait("sync", aS, a_sg)
                ld_wf[ot + 2] = dma("sync", io, xstage[ot % 2][:], wf[ot + 2])
            wait("vector", aS, a_sg)
            wait("vector", tS, t_pf[par])
            v_w2[ot] = op("vector", lambda e, par=par: e.tensor_add(w2slab[par][:], QF[:], KF[:]), vS)
            v_w2_prev[0] = v_w2[ot]

        final_prep(0)
        for ot in range(NCH):
            par = ot % 2
            if ot + 1 < NCH:
                final_prep(ot + 1)
            wait("tensor", vS, v_w2[ot])
            wait("tensor", vS, v_wo[ot])
            for pair in range(2):
                banks = (2 * pair, 2 * pair + 1)
                for sbk in banks:
                    wait_bf("tensor", bank_free_v[sbk])
                    wait("tensor", aS, bank_exp_a[sbk])
                    bank_free_v[sbk] = None
                    bank_exp_a[sbk] = None
                for ic in range(NCH):
                    for sbk in banks:
                        sl = slice(512 * sbk, 512 * sbk + 512)
                        op("tensor", lambda e, ic=ic, sl=sl, sbk=sbk, par=par:
                           e.matmul(P[sbk][:], woslab[par][:, 128 * ic:128 * ic + 128],
                                    AH[:, ic, sl], start=(ic == 0), stop=False))
                t_f = None
                for ic in range(NCH):
                    sp_ = (ic == NCH - 1)
                    for sbk in banks:
                        sl = slice(512 * sbk, 512 * sbk + 512)
                        t_f = op("tensor", lambda e, ic=ic, sl=sl, sbk=sbk, par=par, sp_=sp_:
                                 e.matmul(P[sbk][:], w2slab[par][:, 128 * ic:128 * ic + 128],
                                          XN[:, ic, sl], start=False, stop=sp_),
                                 tS if (sp_ and sbk == banks[1]) else None)
                t_pf[par] = t_f
                for sbk in banks:
                    sl = slice(512 * sbk, 512 * sbk + 512)
                    ob = sbk % 2
                    wait("vector", tS, t_f)
                    wait("vector", io, ob_store[ob])
                    wait("vector", ios, ld_x[ot])
                    wait("vector", io, b_bias)
                    op("vector", lambda e, sbk=sbk, ot=ot, ob=ob:
                       e.tensor_scalar(RRB[:, ob, :], P[sbk][:], S2K[:], bias_sb[:, ot:ot + 1],
                                       AL.mult, AL.add))
                    v_ob[ob] = op("vector", lambda e, sl=sl, ob=ob:
                                  e.tensor_add(RRB[:, ob, :], RRB[:, ob, :], RB[:, sl]), vS)
                    bank_free_v[sbk] = (vS, v_ob[ob])
                    wait("sync", vS, v_ob[ob])
                    ob_store[ob] = dma("sync", io, yt[ot][:, sl], RRB[:, ob, :])
            if ot + 1 < NCH:
                wait("scalar", vS, v_ob[1])
                ld_x[ot + 1] = dma("scalar", ios, RB[:], xt[ot + 1])

        if DEBUG_OUT == "xn":
            wait("gpsimd", vS, vS.n)
            for ic in range(NCH):
                dma("gpsimd", iog, dbg[:, ic, :], XN[:, ic, :])
        elif DEBUG_OUT == "ah":
            wait("gpsimd", vS, vS.n)
            for ic in range(NCH):
                dma("gpsimd", iog, dbg[:, ic, :], AH[:, ic, :])

        wait("sync", io, io.n)
        wait("gpsimd", iog, iog.n)
        wait("scalar", ios, ios.n)

        block = st.enter_context(nc.Block())

        def replay(which):
            def f(e):
                for eng, fn in events:
                    if eng == which:
                        fn(e)
            return f

        block.sync(replay("sync"))
        block.gpsimd(replay("gpsimd"))
        block.vector(replay("vector"))
        block.scalar(replay("scalar"))
        block.tensor(replay("tensor"))

    nc.finalize()
    return nc


_NC_CACHE = None


def kernel(L0_attn_x, L0_attn_Wq, L0_attn_Wk, L0_attn_Wv, L0_attn_Wo,
           L0_ffn_weight, L0_ffn_bias):
    global _NC_CACHE
    x = np.asarray(L0_attn_x)
    B = x.shape[0]

    def wtile(w):
        w = np.asarray(w)
        return np.ascontiguousarray(w.reshape(NH, 128, NCH, 128).transpose(0, 3, 2, 1))

    wqh = wtile(L0_attn_Wq)
    wkh = wtile(L0_attn_Wk)
    wvh = wtile(L0_attn_Wv)
    woh = wtile(L0_attn_Wo)
    wfh = wtile(L0_ffn_weight)
    bias_h = np.ascontiguousarray(np.asarray(L0_ffn_bias).reshape(NCH, 128).T)

    if _NC_CACHE is None:
        _NC_CACHE = build_program()
    nc = _NC_CACHE

    in_maps = []
    for b in range(B):
        xtb = np.ascontiguousarray(x[b].T).reshape(NCH, 128, S)
        in_maps.append({"xt": xtb, "wq": wqh, "wk": wkh, "wv": wvh,
                        "wo": woh, "wf": wfh, "bias_d": bias_h})

    res = run_bass_kernel_spmd(nc, in_maps, core_ids=list(range(B)))
    outs = []
    for b in range(B):
        ytb = np.asarray(res.results[b]["yt"])
        outs.append(ytb.transpose(2, 0, 1).reshape(S, H))
    return np.stack(outs, axis=0)


# version 20
# speedup vs baseline: 1.1618x; 1.0635x over previous
import math
from contextlib import ExitStack

import numpy as np
import ml_dtypes
import concourse.bass as bass
import concourse.mybir as mybir
from concourse.bass_utils import run_bass_kernel_spmd

F32 = mybir.dt.float32
BF16 = mybir.dt.bfloat16
I32 = mybir.dt.int32
AL = mybir.AluOpType
AF = mybir.ActivationFunctionType

S = 2048
H = 2048
NH = 16
HD = 128
NCH = 16
NQB = 4
QS = 512
NSB = 4
SCALE = 1.0 / math.sqrt(HD)
CSHIFT = -20.0
EPS_LN = 1e-5
EPS_FFN = 1e-10
PI = math.pi
TWO_PI = 2.0 * math.pi
_C1 = float(np.float32(6.28125))
_C2 = float(np.float32(np.float64(TWO_PI) - np.float64(_C1)))
_C3 = float(np.float32(np.float64(TWO_PI) - np.float64(_C1) - np.float64(_C2)))
LN1E4_64 = math.log(10000.0) / 64.0

DEBUG_OUT = None


class Sem:
    def __init__(self, handle):
        self.h = handle
        self.n = 0

    def inc(self, k=1):
        self.n += k
        return self.n


def build_program():
    nc = bass.Bass("TRN2", target_bir_lowering=False)

    xt = nc.dram_tensor("xt", [NCH, 128, S], F32, kind="ExternalInput")
    wq = nc.dram_tensor("wq", [NH, 128, NCH, 128], F32, kind="ExternalInput")
    wk = nc.dram_tensor("wk", [NH, 128, NCH, 128], F32, kind="ExternalInput")
    wv = nc.dram_tensor("wv", [NH, 128, NCH, 128], F32, kind="ExternalInput")
    wo = nc.dram_tensor("wo", [NCH, 128, NCH, 128], F32, kind="ExternalInput")
    wf = nc.dram_tensor("wf", [NCH, 128, NCH, 128], F32, kind="ExternalInput")
    bias_d = nc.dram_tensor("bias_d", [128, NCH], F32, kind="ExternalInput")
    yt = nc.dram_tensor("yt", [NCH, 128, S], F32, kind="ExternalOutput")
    dbg = None
    if DEBUG_OUT is not None:
        dbg = nc.dram_tensor("dbg", [128, NCH, S], F32, kind="ExternalOutput")

    events = []

    def ev(eng, fn):
        events.append((eng, fn))

    def wait(eng, sem, v):
        if v is None or v <= 0:
            return
        assert v <= sem.n, f"wait on future value {v} > {sem.n}"
        ev(eng, lambda e, h=sem.h, vv=v: e.wait_ge(h, vv))

    def dma(eng, sem, dst, src, slow=False):
        v = sem.inc(16)
        ev(eng, lambda e, d=dst, s_=src, h=sem.h, sl_=slow:
           e.dma_start(d, s_, allow_slow_non_contiguous=sl_).then_inc(h, 16))
        return v

    def op(eng, fn, sem=None):
        if sem is None:
            ev(eng, fn)
            return None
        v = sem.inc(1)
        ev(eng, lambda e, f=fn, h=sem.h: f(e).then_inc(h, 1))
        return v

    with ExitStack() as st:
        def sb(name, shape, dt):
            return st.enter_context(nc.sbuf_tensor(name, shape, dt))

        def ps(name, shape, dt=F32):
            return st.enter_context(nc.psum_tensor(name, shape, dt))

        XN = sb("XN", [128, NCH, S], BF16)
        AH = sb("AH", [128, NCH, S], BF16)
        WS = sb("WS", [128, S], F32)
        WS2 = sb("WS2", [128, S], F32)
        RB = sb("RB", [128, S], F32)
        WQB = sb("WQB", [128, S], BF16)
        WKB = sb("WKB", [128, S], BF16)
        WVB = sb("WVB", [128, S], BF16)
        QR = sb("QR", [128, S], BF16)
        QSW = sb("QSW", [128, S], BF16)
        QF = sb("QF", [128, S], BF16)
        KF = sb("KF", [128, S], BF16)
        VH = sb("VH", [128, NCH, 128], BF16)
        PT = sb("PT", [128, 4, QS], BF16)
        PTX = sb("PTX", [128, 2, 128], BF16)
        PTP = sb("PTP", [128, 2, QS], BF16)
        VT = sb("VT", [128, 1024], BF16)
        RRB = sb("RRB", [128, 2, QS], F32)
        KI = sb("KI", [128, 256], I32)
        TA = sb("TA", [128, S], BF16)
        TB = sb("TB", [128, S], BF16)
        onescol = sb("onescol", [128, 1], BF16)
        onescolf = sb("onescolf", [128, 1], F32)
        ones1 = sb("ones1", [1, 128], F32)
        ones1b = sb("ones1b", [1, 128], BF16)
        tri = sb("tri", [128, 128], BF16)
        ident = sb("ident", [128, 128], BF16)
        bias_sb = sb("bias_sb", [128, NCH], F32)
        J64 = sb("J64", [128, 1], F32)
        INV = sb("INV", [128, 1], F32)
        SGN = sb("SGN", [128, 1], F32)
        NEGC = sb("NEGC", [128, 1], F32)
        ACC = sb("ACC", [128, 1], F32)
        ACC2 = sb("ACC2", [128, 1], F32)
        SVAL = sb("SVAL", [1, 8], F32)
        TPOS = sb("TPOS", [128, 1], F32)
        TNEG = sb("TNEG", [128, 1], F32)
        S2K = sb("S2K", [128, 1], F32)
        I2S = sb("I2S", [128, 1], F32)

        PSC = ps("PSC", [128, 2048])

        class BankView:
            def __init__(self, base, off):
                self.base = base
                self.off = off

            def __getitem__(self, idx):
                if idx == slice(None):
                    return self.base[:, self.off:self.off + 512]
                p, c = idx
                assert p == slice(None)
                return self.base[:, self.off + (c.start or 0):self.off + c.stop]

        P = [BankView(PSC, 512 * i) for i in range(4)]
        PS4 = ps("PS4", [128, 512])
        PS5B = ps("PS5B", [128, 1024], BF16)
        PPV = ps("PPV", [128, 512])
        PL = ps("PL", [128, 512])

        io = Sem(st.enter_context(nc.semaphore("io")))
        iog = Sem(st.enter_context(nc.semaphore("iog")))
        ios = Sem(st.enter_context(nc.semaphore("ios")))
        tS = Sem(st.enter_context(nc.semaphore("tS")))
        vS = Sem(st.enter_context(nc.semaphore("vS")))
        aS = Sem(st.enter_context(nc.semaphore("aS")))
        gS = Sem(st.enter_context(nc.semaphore("gS")))

        xstage = [WS, WS2]

        def gp_setup(e):
            e.iota(WS2[:], [[1, S]], channel_multiplier=0, allow_small_or_imprecise_dtypes=True)
            e.iota(J64[0:64, :], [[0, 1]], channel_multiplier=1, allow_small_or_imprecise_dtypes=True)
            e.iota(J64[64:128, :], [[0, 1]], channel_multiplier=1, allow_small_or_imprecise_dtypes=True)
            e.memset(SGN[0:64, :], -1.0)
            e.memset(SGN[64:128, :], 1.0)
            e.memset(onescol[:], 1.0)
            e.memset(onescolf[:], 1.0)
            e.memset(ones1[:], 1.0)
            e.memset(ones1b[:], 1.0)
            e.memset(NEGC[:], CSHIFT)
            e.memset(SVAL[0:1, 5:6], EPS_LN)
            e.memset(SVAL[0:1, 6:7], math.log(2.0))
            e.memset(QF[:, 0:128], 1.0)
            e.affine_select(tri[:], QF[:, 0:128], [[1, 128]], AL.is_ge, 0.0,
                            base=0, channel_multiplier=-1)
            return e.affine_select(ident[:], QF[:, 0:128], [[1, 128]], AL.is_equal, 0.0,
                                   base=0, channel_multiplier=-1)
        g_setup = op("gpsimd", gp_setup, gS)
        b_bias = dma("sync", io, bias_sb[:], bias_d[:])

        ld_w = [[None, None, None] for _ in range(NH)]
        wait("gpsimd", gS, g_setup)

        wait("scalar", gS, g_setup)
        a_inv = op("scalar", lambda e: e.activation(INV[:], J64[:], AF.Exp, scale=-LN1E4_64), aS)
        wait("vector", gS, g_setup)
        wait("vector", aS, a_inv)
        op("vector", lambda e: e.tensor_scalar(RB[:], WS2[:], INV[:], None, AL.mult))
        for cc in range(8):
            sl = slice(256 * cc, 256 * cc + 256)
            op("vector", lambda e, sl=sl: e.tensor_scalar(WS2[:, sl], RB[:, sl], 1.0 / TWO_PI, None, AL.mult))
            op("vector", lambda e, sl=sl: e.tensor_copy(KI[:], WS2[:, sl]))
            op("vector", lambda e, sl=sl: e.tensor_copy(WS2[:, sl], KI[:]))
        op("vector", lambda e: e.scalar_tensor_tensor(WS[:], WS2[:], _C1, RB[:], AL.mult, AL.subtract))
        op("vector", lambda e: e.scalar_tensor_tensor(RB[:], WS2[:], _C2, WS[:], AL.mult, AL.add))
        op("vector", lambda e: e.scalar_tensor_tensor(WS[:], WS2[:], _C3, RB[:], AL.mult, AL.add))
        op("vector", lambda e: e.tensor_scalar(WS2[:], WS[:], -PI, TWO_PI, AL.is_lt, AL.mult))
        op("vector", lambda e: e.tensor_add(RB[:], WS[:], WS2[:]))
        op("vector", lambda e: e.tensor_scalar(WS2[:], RB[:], PI, TWO_PI, AL.is_gt, AL.mult))
        v_rs = op("vector", lambda e: e.tensor_sub(WS[:], RB[:], WS2[:]), vS)
        wait("scalar", vS, v_rs)
        a_sin = op("scalar", lambda e: e.activation(TA[:], WS[:], AF.Sin, scale=-1.0), aS)
        wait("vector", aS, a_sin)
        op("vector", lambda e: e.tensor_scalar(RB[:], WS[:], PI / 2.0, None, AL.add))
        op("vector", lambda e: e.tensor_scalar(WS2[:], RB[:], PI, TWO_PI, AL.is_gt, AL.mult))
        v_rc = op("vector", lambda e: e.tensor_sub(WS[:], RB[:], WS2[:]), vS)
        wait("scalar", vS, v_rc)
        a_cos = op("scalar", lambda e: e.activation(QR[:], WS[:], AF.Sin), aS)
        wait("vector", aS, a_cos)
        v_tab = op("vector", lambda e: e.tensor_scalar(TB[:], QR[:], SGN[:], None, AL.mult), vS)

        d_x = [None] * NCH
        for ic in range(NCH):
            d_x[ic] = dma("gpsimd", iog, AH[:, ic, :], xt[ic])
        ld_w[0][0] = dma("gpsimd", iog, WQB[:], wq[0])
        ld_w[0][1] = dma("gpsimd", iog, WKB[:], wk[0])
        ld_w[0][2] = dma("gpsimd", iog, WVB[:], wv[0])
        xn_marks = [[None] * NCH for _ in range(NSB)]
        apply_sub_last = [None]
        dq_x = [None] * 64
        wait("sync", vS, v_tab)
        for q in range(8):
            slot = q % 8
            dst = [WS, WS2][slot // 4][:, 512 * (slot % 4):512 * (slot % 4) + 512]
            dq_x[q] = dma("sync", io, dst, xt[q][:, 0:512])
        v_ep = None
        ep_mark = None
        mbc_marks = [None] * NSB
        rbev_marks = [None] * NSB
        for sbk in range(NSB):
            sl = slice(512 * sbk, 512 * sbk + 512)
            sq_marks = [None, None]
            for ic in range(NCH):
                wait("scalar", tS, sq_marks[ic % 2])
                wait("scalar", iog, d_x[ic])
                if ic == 0 and v_ep is not None:
                    wait("scalar", vS, v_ep)
                a_sq = op("scalar", lambda e, ic=ic, sl=sl:
                          e.activation(PT[:, ic % 2, :], AH[:, ic, sl], AF.Square), aS)
                wait("tensor", aS, a_sq)
                st_, sp_ = (ic == 0), (ic == NCH - 1)
                wait("tensor", iog, d_x[ic])
                if ic == 0 and ep_mark is not None:
                    wait("tensor", vS, ep_mark)
                op("tensor", lambda e, ic=ic, sl=sl, st_=st_, sp_=sp_:
                   e.matmul(PSC[0:1, 0:512], onescol[:], AH[:, ic, sl], start=st_, stop=sp_))
                sq_marks[ic % 2] = op("tensor", lambda e, ic=ic, st_=st_, sp_=sp_:
                                      e.matmul(PSC[0:1, 512:1024], onescol[:], PT[:, ic % 2, :],
                                               start=st_, stop=sp_), tS)
            t_mm = sq_marks[(NCH - 1) % 2]
            wait("vector", tS, t_mm)
            op("vector", lambda e: e.tensor_scalar(RRB[0:1, 1, :], PSC[0:1, 0:512], 1.0 / H, None, AL.mult))
            op("vector", lambda e: e.tensor_scalar(RRB[0:1, 0, :], PSC[0:1, 512:1024], 1.0 / H, None, AL.mult))
            op("vector", lambda e: e.tensor_mul(PSC[0:1, 0:512], RRB[0:1, 1, :], RRB[0:1, 1, :]))
            v_var = op("vector", lambda e: e.tensor_sub(RRB[0:1, 0, :], RRB[0:1, 0, :], PSC[0:1, 0:512]), vS)
            ep_mark = v_var
            wait("scalar", vS, v_var)
            op("scalar", lambda e: e.activation(RRB[0:1, 0, :], RRB[0:1, 0, :], AF.Ln, bias=SVAL[0:1, 5:6]))
            a_r = op("scalar", lambda e: e.activation(RRB[0:1, 0, :], RRB[0:1, 0, :], AF.Exp, scale=-0.5), aS)
            wait("tensor", aS, a_r)
            if sbk > 0:
                wait("tensor", vS, apply_sub_last[0])
            mbc_marks[sbk] = op("tensor", lambda e: e.matmul(PS4[:], ones1[:], RRB[0:1, 1, :],
                                                             start=True, stop=True), tS)
            t_r = op("tensor", lambda e: e.matmul(PL[:], ones1[:], RRB[0:1, 0, :],
                                                  start=True, stop=True), tS)
            wait("vector", tS, t_r)
            v_ep = op("vector", lambda e, sl=sl: e.tensor_copy(RB[:, sl], PL[:]), vS)
            rbev_marks[sbk] = v_ep
            wait("tensor", vS, v_ep)

            for ic in range(NCH):
                q = 16 * sbk + ic
                slot = q % 8
                qsl = [WS, WS2][slot // 4][:, 512 * (slot % 4):512 * (slot % 4) + 512]
                wait("vector", io, dq_x[q])
                wait("vector", tS, mbc_marks[sbk])
                op("vector", lambda e, qsl=qsl: e.tensor_sub(RRB[:, 1, :], qsl, PS4[:]))
                xn_marks[sbk][ic] = op("vector", lambda e, ic=ic, sl=sl:
                                       e.tensor_mul(XN[:, ic, sl], RRB[:, 1, :], RB[:, sl]), vS)
                apply_sub_last[0] = xn_marks[sbk][ic]
                if q + 8 < 64:
                    wait("sync", vS, xn_marks[sbk][ic])
                    q2 = q + 8
                    slot2 = q2 % 8
                    dst2 = [WS, WS2][slot2 // 4][:, 512 * (slot2 % 4):512 * (slot2 % 4) + 512]
                    dq_x[q2] = dma("sync", io, dst2,
                                   xt[q2 % 16][:, slice(512 * (q2 // 16), 512 * (q2 // 16) + 512)])
        v_apply = xn_marks[NSB - 1][NCH - 1]

        ffn_d = [None] * NCH
        ffn_abs = [None] * NCH
        ffn_acc_v = [None]

        def emit_ffn_stat(ot):
            if ot < 2:
                wait("sync", vS, v_apply)
            else:
                wait("sync", aS, ffn_abs[ot - 2])
            ffn_d[ot] = dma("sync", io, xstage[ot % 2][:], wf[ot])
            wait("scalar", io, ffn_d[ot])
            for pc in range(4):
                psl = slice(512 * pc, 512 * pc + 512)
                first = (ot == 0 and pc == 0)
                acc_t = ACC if first else (ACC2 if pc % 2 == 0 else TPOS)
                if not first:
                    wait("scalar", vS, ffn_acc_v[0])
                ffn_abs[ot] = op("scalar", lambda e, ot=ot, psl=psl, acc_t=acc_t:
                                 e.activation(RRB[:, 1, :], xstage[ot % 2][:, psl], AF.Abs,
                                              accum_out=acc_t[:]), aS)
                if not first:
                    wait("vector", aS, ffn_abs[ot])
                    ffn_acc_v[0] = op("vector", lambda e, acc_t=acc_t:
                                      e.tensor_add(ACC[:], ACC[:], acc_t[:]), vS)

        bank_free_v = [None] * 4
        bank_exp_a = [None] * 4

        def wait_bf(eng, bf):
            if bf is not None:
                wait(eng, bf[0], bf[1])
        exp_marks = {}
        pv_marks = {}
        vm_marks = {}
        vpair_marks = {}
        lpair_marks = {}
        chunk_info = {}
        g_ctr = 0
        v_ah_last = None
        t_bc_pending = None

        def proj(wbuf, marks, ldmark, mid=None, gates=None):
            wait("tensor", iog, ldmark)
            for pair in range(2):
                if pair == 1 and mid is not None:
                    mid()
                banks = (2 * pair, 2 * pair + 1)
                for sbk in banks:
                    wait_bf("tensor", bank_free_v[sbk])
                    wait("tensor", aS, bank_exp_a[sbk])
                    bank_free_v[sbk] = None
                    bank_exp_a[sbk] = None
                for ic in range(NCH):
                    st_, sp_ = (ic == 0), (ic == NCH - 1)
                    for sbk in banks:
                        sl = slice(512 * sbk, 512 * sbk + 512)
                        if gates is not None:
                            wait("tensor", vS, gates[sbk][ic])
                        m = op("tensor",
                               lambda e, wbuf=wbuf, ic=ic, sl=sl, sbk=sbk, st_=st_, sp_=sp_:
                               e.matmul(P[sbk][:], wbuf[:, 128 * ic:128 * ic + 128],
                                        XN[:, ic, sl], start=st_, stop=sp_),
                               tS if (sp_ and sbk == banks[1]) else None)
                        if sp_:
                            marks[sbk] = None
                marks[2 * pair] = m
                marks[2 * pair + 1] = m

        def evac_rope(dstF, marks, prev_rope=None):
            wait("scalar", vS, prev_rope)
            av = None
            for sbk in range(NSB):
                sl = slice(512 * sbk, 512 * sbk + 512)
                wait("scalar", tS, marks[sbk])
                av = op("scalar", lambda e, sl=sl, sbk=sbk: e.activation(QR[:, sl], P[sbk][:], AF.Copy), aS)
                bank_free_v[sbk] = (aS, av)
            wait("gpsimd", aS, av)
            dma("gpsimd", iog, QSW[0:64, :], QR[64:128, :])
            dsw = dma("gpsimd", iog, QSW[64:128, :], QR[0:64, :])
            wait("vector", aS, av)
            op("vector", lambda e, dstF=dstF: e.tensor_mul(dstF[:], QR[:], TA[:]))
            wait("vector", iog, dsw)
            op("vector", lambda e: e.tensor_mul(QR[:], QSW[:], TB[:]))
            return op("vector", lambda e, dstF=dstF: e.tensor_add(dstF[:], dstF[:], QR[:]), vS)

        vh_gate = [None]
        rr_guard = [None]
        P3V = P[3]
        PVACC = [PPV, P3V]
        qb_parity = [0]
        ppv_free = [None, None]

        def emit_pv(g, ppv_cur):
            tc, slot, ntc, is_diag, st_, sp_ = chunk_info[g]
            if is_diag:
                wait("tensor", vS, vm_marks[g])
            else:
                wait("tensor", aS, exp_marks[g])
            if tc == 8:
                wait("tensor", vS, vh_gate[0])
            if st_:
                wait_bf("tensor", ppv_free[ppv_cur is P3V])
                if ppv_cur is P3V:
                    wait_bf("tensor", bank_free_v[3])
                    bank_free_v[3] = None
            pv_marks[g] = op("tensor", lambda e, tc=tc, slot=slot, st_=st_, sp_=sp_, pc=ppv_cur:
                             e.matmul(pc[:], VH[:, tc, :], PT[:, slot, :], start=st_, stop=sp_),
                             tS)
            if tc % 2 == 1:
                p = g // 2
                pi = p % 2
                wait("tensor", vS, vpair_marks[g])
                if tc == 1:
                    wait("tensor", aS, rr_guard[0])
                lpair_marks[p] = op("tensor", lambda e, pi=pi, st2=(tc == 1), sp_=sp_:
                                    e.matmul(PL[0:1, :], onescol[:], PTP[:, pi, :],
                                             start=st2, stop=sp_), tS)

        def emit_bc(pend):
            nonlocal v_ah_last
            a_rr_p, qb_p, h_p, pc = pend
            wait("tensor", aS, a_rr_p)
            t_bc = op("tensor", lambda e: e.matmul(PS4[:], ones1[:], RRB[0:1, 0, :],
                                                   start=True, stop=True), tS)
            wait("vector", tS, t_bc)
            op("vector", lambda e: e.tensor_copy(RRB[:, 0, :], PS4[:]))
            qsl_p = slice(QS * qb_p, QS * qb_p + QS)
            v_ah_last = op("vector", lambda e, h_p=h_p, qsl_p=qsl_p, pc=pc:
                           e.tensor_mul(AH[:, h_p, qsl_p], pc[:], RRB[:, 0, :]), vS)
            ppv_free[pc is P3V] = (vS, v_ah_last)
            if pc is P3V:
                bank_free_v[3] = (vS, v_ah_last)

        bank_free_v[0] = (vS, ep_mark)
        bank_free_v[1] = (vS, ep_mark)

        for h in range(NH):
            pend = t_bc_pending
            t_bc_pending = None

            def bc_mid(pend=pend):
                if pend is not None:
                    emit_bc(pend)
            tq = [None] * NSB
            proj(WQB, tq, ld_w[h][0], mid=bc_mid,
                 gates=(xn_marks if h == 0 else None))
            v_qf = evac_rope(QF, tq)
            if h + 1 < NH:
                wait("gpsimd", tS, tq[NSB - 1])
                ld_w[h + 1][0] = dma("gpsimd", iog, WQB[:], wq[h + 1])
            if h < NCH:
                emit_ffn_stat(h)
            tk = [None] * NSB
            proj(WKB, tk, ld_w[h][1])
            v_kf = evac_rope(KF, tk, prev_rope=v_qf)
            if h + 1 < NH:
                wait("gpsimd", tS, tk[NSB - 1])
                ld_w[h + 1][1] = dma("gpsimd", iog, WKB[:], wk[h + 1])
            wait("tensor", iog, ld_w[h][2])
            v_vh = None
            v_vt = None
            t_tr = None
            for half in range(2):
                banks = (2 * half, 2 * half + 1)
                for sbk in banks:
                    wait_bf("tensor", bank_free_v[sbk])
                    wait("tensor", aS, bank_exp_a[sbk])
                    bank_free_v[sbk] = None
                    bank_exp_a[sbk] = None
                t_vm = None
                for ic in range(NCH):
                    st_, sp_ = (ic == 0), (ic == NCH - 1)
                    if half == 1 and 2 <= ic < 10:
                        j = ic - 2
                        if ic == 2:
                            wait_bf("tensor", v_vt)
                        t_tr = op("tensor", lambda e, j=j:
                                  e.transpose(PS5B[:, 128 * j:128 * j + 128],
                                              VT[:, 128 * j:128 * j + 128], ident[:]),
                                  tS if j == 7 else None)
                    for sbk in banks:
                        sl = slice(512 * sbk, 512 * sbk + 512)
                        t_vm = op("tensor",
                                  lambda e, ic=ic, sl=sl, sbk=sbk, st_=st_, sp_=sp_:
                                  e.matmul(P[sbk][:], WVB[:, 128 * ic:128 * ic + 128],
                                           XN[:, ic, sl], start=st_, stop=sp_),
                                  tS if (sp_ and sbk == banks[1]) else None)
                if half == 0:
                    for j in range(2):
                        sbk = banks[j]
                        sl1 = slice(512 * j, 512 * j + 512)
                        wait("scalar", tS, t_vm)
                        v_vt = op("scalar", lambda e, sl1=sl1, sbk=sbk:
                                  e.activation(VT[:, sl1], P[sbk][:], AF.Copy), aS)
                        bank_free_v[sbk] = (aS, v_vt)
                        v_vt = (aS, v_vt)
                else:
                    wait("scalar", tS, t_tr)
                    v_vh = op("scalar", lambda e:
                              e.activation(VH[:, 0:8, :], PS5B[:], AF.Copy), aS)
                    v_vh = (aS, v_vh)
                    for j in range(2):
                        sbk = banks[j]
                        sl1 = slice(512 * j, 512 * j + 512)
                        wait("scalar", tS, t_vm)
                        v_vt = op("scalar", lambda e, sl1=sl1, sbk=sbk:
                                  e.activation(VT[:, sl1], P[sbk][:], AF.Copy), aS)
                        bank_free_v[sbk] = (aS, v_vt)
                        v_vt = (aS, v_vt)
            if h + 1 < NH:
                wait("gpsimd", tS, t_vm)
                ld_w[h + 1][2] = dma("gpsimd", iog, WVB[:], wv[h + 1])
            pend_tr = (v_vt, v_vh)

            vh_gate[0] = None
            wait("tensor", vS, v_qf)
            wait("tensor", vS, v_kf)
            for qb in range(NQB):
                qsl = slice(QS * qb, QS * qb + QS)
                ntc = 4 * qb + 4
                qb_g0 = g_ctr
                ppv_cur = PVACC[qb_parity[0]]
                qb_parity[0] ^= 1
                for tc in range(ntc):
                    g = g_ctr
                    g_ctr += 1
                    bank = g % 3
                    wait("tensor", aS, exp_marks.get(g - 3))
                    wait_bf("tensor", bank_free_v[bank])
                    bank_free_v[bank] = None
                    t_sc = op("tensor", lambda e, tc=tc, qsl=qsl, bank=bank:
                              e.matmul(P[bank][:], KF[:, 128 * tc:128 * tc + 128], QF[:, qsl],
                                       start=True, stop=True), tS)
                    slot = g % 4
                    j = tc - 4 * qb
                    is_diag = j >= 0
                    wait("scalar", tS, t_sc)
                    wait("scalar", tS, pv_marks.get(g - 4))
                    gp4 = g - 4
                    if gp4 >= 0 and (gp4 | 1) in vpair_marks:
                        wait("scalar", vS, vpair_marks[gp4 | 1])
                    if j > 0:
                        op("scalar", lambda e, slot=slot, j=j: e.memzero(PT[:, slot, 0:128 * j]))
                    if is_diag:
                        px = g % 2
                        wait("scalar", vS, vm_marks.get(g - 2))
                        if j < 3:
                            op("scalar", lambda e, px=px, j=j, bank=bank:
                               e.activation(PTX[:, px, :], P[bank][:, 128 * j:128 * j + 128],
                                            AF.Exp, bias=NEGC[:], scale=SCALE))
                            a_e = op("scalar", lambda e, slot=slot, j=j, bank=bank:
                                     e.activation(PT[:, slot, 128 * (j + 1):QS],
                                                  P[bank][:, 128 * (j + 1):QS],
                                                  AF.Exp, bias=NEGC[:], scale=SCALE), aS)
                        else:
                            a_e = op("scalar", lambda e, px=px, j=j, bank=bank:
                                     e.activation(PTX[:, px, :], P[bank][:, 128 * j:128 * j + 128],
                                                  AF.Exp, bias=NEGC[:], scale=SCALE), aS)
                        wait("vector", aS, a_e)
                        vm_marks[g] = op("vector", lambda e, slot=slot, j=j, px=px:
                                         e.tensor_mul(PT[:, slot, 128 * j:128 * j + 128],
                                                      PTX[:, px, :], tri[:]), vS)
                    else:
                        a_e = op("scalar", lambda e, slot=slot, bank=bank:
                                 e.activation(PT[:, slot, :], P[bank][:], AF.Exp,
                                              bias=NEGC[:], scale=SCALE), aS)
                    exp_marks[g] = a_e
                    bank_exp_a[bank] = a_e
                    chunk_info[g] = (tc, slot, ntc, is_diag, tc == 0, tc == ntc - 1)
                    if tc % 2 == 1:
                        p = g // 2
                        pi = p % 2
                        for gg in (g - 1, g):
                            if chunk_info[gg][3]:
                                wait("vector", vS, vm_marks[gg])
                            else:
                                wait("vector", aS, exp_marks[gg])
                        wait("vector", tS, lpair_marks.get(p - 2))
                        s0 = chunk_info[g - 1][1]
                        vpair_marks[g] = op("vector", lambda e, pi=pi, s0=s0, s1=slot:
                                            e.tensor_add(PTP[:, pi, :], PT[:, s0, :],
                                                         PT[:, s1, :]), vS)
                    if tc == min(5, ntc - 1) and t_bc_pending is not None:
                        emit_bc(t_bc_pending)
                        t_bc_pending = None
                    if qb == 0 and tc in (1, 2, 3) and pend_tr is not None:
                        v_vt_p, v_vh_p = pend_tr
                        if tc == 1:
                            wait_bf("tensor", v_vt_p)
                            wait_bf("tensor", v_vh_p)
                        for jj in range(3 * (tc - 1), min(3 * tc, 8)):
                            t_tr2 = op("tensor", lambda e, jj=jj:
                                       e.transpose(PS5B[:, 128 * jj:128 * jj + 128],
                                                   VT[:, 128 * jj:128 * jj + 128], ident[:]),
                                       tS if jj == 7 else None)
                        if tc == 3:
                            pend_tr = None
                            wait("vector", tS, t_tr2)
                            v_vh2 = op("vector", lambda e:
                                       e.tensor_copy(VH[:, 8:16, :], PS5B[:]), vS)
                            vh_gate[0] = v_vh2
                    if tc >= 3:
                        emit_pv(g - 3, ppv_cur)
                for gg in range(max(qb_g0, g_ctr - 3), g_ctr):
                    emit_pv(gg, ppv_cur)
                wait("scalar", tS, lpair_marks[(g_ctr - 1) // 2])
                op("scalar", lambda e: e.activation(RRB[0:1, 0, :], PL[0:1, :], AF.Ln))
                a_rr = op("scalar", lambda e: e.activation(RRB[0:1, 0, :], RRB[0:1, 0, :], AF.Exp, scale=-1.0), aS)
                rr_guard[0] = a_rr
                t_bc_pending = (a_rr, qb, h, ppv_cur)

            if DEBUG_OUT == "h0" and h == 0:
                wait("gpsimd", vS, max(v_qf, v_kf))
                wait("gpsimd", aS, aS.n)
                dma("gpsimd", iog, dbg[:, 0, :], QF[:])
                dma("gpsimd", iog, dbg[:, 1, :], KF[:])
                dma("gpsimd", iog, dbg[:, 2, :], VH[:])

        a_rr_p = t_bc_pending[0]
        emit_bc(t_bc_pending)
        t_bc_pending = None

        wait("tensor", vS, ffn_acc_v[0])
        wait("tensor", aS, a_rr_p)
        t_s = op("tensor", lambda e: e.matmul(PL[0:1, 0:1], ACC[:], onescolf[:], start=True, stop=True), tS)
        wait("scalar", tS, t_s)
        a_s = op("scalar", lambda e: e.activation(SVAL[:, 0:1], PL[0:1, 0:1], AF.Copy,
                                                  scale=1.0 / float(H * H)), aS)
        op("scalar", lambda e: e.activation(SVAL[:, 4:5], SVAL[:, 0:1], AF.Ln))
        a_i2s = op("scalar", lambda e: e.activation(SVAL[:, 2:3], SVAL[:, 4:5], AF.Exp,
                                                    bias=SVAL[0:1, 6:7], scale=-1.0), aS)
        wait("vector", aS, a_i2s)
        op("vector", lambda e: e.tensor_scalar(SVAL[:, 1:2], SVAL[:, 0:1], 0.5, 0.5 * EPS_FFN, AL.mult, AL.add))
        v_sv = op("vector", lambda e: e.tensor_scalar(SVAL[:, 3:4], SVAL[:, 0:1], 0.5, None, AL.mult), vS)
        wait("tensor", vS, v_sv)
        op("tensor", lambda e: e.matmul(PL[:, 0:1], ones1[:], SVAL[0:1, 1:2], start=True, stop=True))
        op("tensor", lambda e: e.matmul(PL[:, 1:2], ones1[:], SVAL[0:1, 2:3], start=True, stop=True))
        t_b3 = op("tensor", lambda e: e.matmul(PL[:, 2:3], ones1[:], SVAL[0:1, 3:4], start=True, stop=True), tS)
        wait("vector", tS, t_b3)
        op("vector", lambda e: e.tensor_copy(TPOS[:], PL[:, 0:1]))
        op("vector", lambda e: e.tensor_scalar(TNEG[:], PL[:, 0:1], -1.0, None, AL.mult))
        op("vector", lambda e: e.tensor_copy(I2S[:], PL[:, 1:2]))
        v_cst = op("vector", lambda e: e.tensor_copy(S2K[:], PL[:, 2:3]), vS)

        ld_wo = [None] * NCH
        ld_wf = [None] * NCH
        ld_x = [None] * NCH
        wait("gpsimd", vS, v_ah_last)
        ld_wo[0] = dma("gpsimd", iog, WVB[:], wo[0])
        wait("sync", aS, ffn_abs[NCH - 1])
        ld_wf[0] = dma("sync", io, WS[:], wf[0])
        ld_wf[1] = dma("sync", io, WS2[:], wf[1])
        wait("scalar", vS, v_apply)
        ld_x[0] = dma("scalar", ios, RB[:], xt[0])

        woslab = [WQB, QR]
        w2slab = [WKB, QSW]
        sgslab = [QF, KF]
        t_pf = [t_s, t_s]
        v_w2 = [None] * NCH
        v_wo = [None] * NCH
        v_w2_prev = [None]
        ob_store = [None, None]
        v_ob = [None, None]

        def final_prep(ot):
            par = ot % 2
            wait("vector", iog, ld_wo[ot])
            wait("vector", tS, t_pf[par])
            v_wo[ot] = op("vector", lambda e, par=par:
                          e.tensor_scalar(woslab[par][:], WVB[:], I2S[:], None, AL.mult), vS)
            if ot + 1 < NCH:
                wait("gpsimd", vS, v_wo[ot])
                ld_wo[ot + 1] = dma("gpsimd", iog, WVB[:], wo[ot + 1])
            wait("scalar", io, ld_wf[ot])
            wait("scalar", vS, v_w2_prev[0] if v_w2_prev[0] is not None else v_cst)
            op("scalar", lambda e, ot=ot: e.activation(QF[:], xstage[ot % 2][:], AF.Sign, bias=TNEG[:]))
            a_sg = op("scalar", lambda e, ot=ot: e.activation(KF[:], xstage[ot % 2][:], AF.Sign, bias=TPOS[:]), aS)
            if ot + 2 < NCH:
                wait("sync", aS, a_sg)
                ld_wf[ot + 2] = dma("sync", io, xstage[ot % 2][:], wf[ot + 2])
            wait("vector", aS, a_sg)
            wait("vector", tS, t_pf[par])
            v_w2[ot] = op("vector", lambda e, par=par: e.tensor_add(w2slab[par][:], QF[:], KF[:]), vS)
            v_w2_prev[0] = v_w2[ot]

        final_prep(0)
        for ot in range(NCH):
            par = ot % 2
            if ot + 1 < NCH:
                final_prep(ot + 1)
            wait("tensor", vS, v_w2[ot])
            wait("tensor", vS, v_wo[ot])
            for pair in range(2):
                banks = (2 * pair, 2 * pair + 1)
                for sbk in banks:
                    wait_bf("tensor", bank_free_v[sbk])
                    wait("tensor", aS, bank_exp_a[sbk])
                    bank_free_v[sbk] = None
                    bank_exp_a[sbk] = None
                for ic in range(NCH):
                    for sbk in banks:
                        sl = slice(512 * sbk, 512 * sbk + 512)
                        op("tensor", lambda e, ic=ic, sl=sl, sbk=sbk, par=par:
                           e.matmul(P[sbk][:], woslab[par][:, 128 * ic:128 * ic + 128],
                                    AH[:, ic, sl], start=(ic == 0), stop=False))
                t_f = None
                for ic in range(NCH):
                    sp_ = (ic == NCH - 1)
                    for sbk in banks:
                        sl = slice(512 * sbk, 512 * sbk + 512)
                        t_f = op("tensor", lambda e, ic=ic, sl=sl, sbk=sbk, par=par, sp_=sp_:
                                 e.matmul(P[sbk][:], w2slab[par][:, 128 * ic:128 * ic + 128],
                                          XN[:, ic, sl], start=False, stop=sp_),
                                 tS if (sp_ and sbk == banks[1]) else None)
                t_pf[par] = t_f
                for sbk in banks:
                    sl = slice(512 * sbk, 512 * sbk + 512)
                    ob = sbk % 2
                    wait("vector", tS, t_f)
                    wait("vector", io, ob_store[ob])
                    wait("vector", ios, ld_x[ot])
                    wait("vector", io, b_bias)
                    op("vector", lambda e, sbk=sbk, ot=ot, ob=ob:
                       e.tensor_scalar(RRB[:, ob, :], P[sbk][:], S2K[:], bias_sb[:, ot:ot + 1],
                                       AL.mult, AL.add))
                    v_ob[ob] = op("vector", lambda e, sl=sl, ob=ob:
                                  e.tensor_add(RRB[:, ob, :], RRB[:, ob, :], RB[:, sl]), vS)
                    bank_free_v[sbk] = (vS, v_ob[ob])
                    wait("sync", vS, v_ob[ob])
                    ob_store[ob] = dma("sync", io, yt[ot][:, sl], RRB[:, ob, :])
            if ot + 1 < NCH:
                wait("scalar", vS, v_ob[1])
                ld_x[ot + 1] = dma("scalar", ios, RB[:], xt[ot + 1])

        if DEBUG_OUT == "xn":
            wait("gpsimd", vS, vS.n)
            for ic in range(NCH):
                dma("gpsimd", iog, dbg[:, ic, :], XN[:, ic, :])
        elif DEBUG_OUT == "ah":
            wait("gpsimd", vS, vS.n)
            for ic in range(NCH):
                dma("gpsimd", iog, dbg[:, ic, :], AH[:, ic, :])

        wait("sync", io, io.n)
        wait("gpsimd", iog, iog.n)
        wait("scalar", ios, ios.n)

        block = st.enter_context(nc.Block())

        def replay(which):
            def f(e):
                for eng, fn in events:
                    if eng == which:
                        fn(e)
            return f

        block.sync(replay("sync"))
        block.gpsimd(replay("gpsimd"))
        block.vector(replay("vector"))
        block.scalar(replay("scalar"))
        block.tensor(replay("tensor"))

    nc.finalize()
    return nc


_NC_CACHE = None


def kernel(L0_attn_x, L0_attn_Wq, L0_attn_Wk, L0_attn_Wv, L0_attn_Wo,
           L0_ffn_weight, L0_ffn_bias):
    global _NC_CACHE
    x = np.asarray(L0_attn_x)
    B = x.shape[0]

    def wtile(w):
        w = np.asarray(w)
        return np.ascontiguousarray(w.reshape(NH, 128, NCH, 128).transpose(0, 3, 2, 1))

    wqh = wtile(L0_attn_Wq)
    wkh = wtile(L0_attn_Wk)
    wvh = wtile(L0_attn_Wv)
    woh = wtile(L0_attn_Wo)
    wfh = wtile(L0_ffn_weight)
    bias_h = np.ascontiguousarray(np.asarray(L0_ffn_bias).reshape(NCH, 128).T)

    if _NC_CACHE is None:
        _NC_CACHE = build_program()
    nc = _NC_CACHE

    in_maps = []
    for b in range(B):
        xtb = np.ascontiguousarray(x[b].T).reshape(NCH, 128, S)
        in_maps.append({"xt": xtb, "wq": wqh, "wk": wkh, "wv": wvh,
                        "wo": woh, "wf": wfh, "bias_d": bias_h})

    res = run_bass_kernel_spmd(nc, in_maps, core_ids=list(range(B)))
    outs = []
    for b in range(B):
        ytb = np.asarray(res.results[b]["yt"])
        outs.append(ytb.transpose(2, 0, 1).reshape(S, H))
    return np.stack(outs, axis=0)
